# version 13
# baseline (speedup 1.0000x reference)
import contextlib
import numpy as np
import ml_dtypes
import concourse.bass as bass
import concourse.mybir as mybir
from concourse.bass_utils import run_bass_kernel_spmd

F32 = mybir.dt.float32
BF16 = mybir.dt.bfloat16
AF = mybir.ActivationFunctionType
ALU = mybir.AluOpType
AX = mybir.AxisListType

P = 128
D = 2048
DC = D // P
NT = 4096
TT = 512
NTILE = NT // TT
NG = TT // P
NH = 16
DH = 128
S = 2048
NSEQ = 2
NE = 9
H = 1024
HC = H // P
QH = 256
INV_D = 1.0 / D
EPS = 1e-5
ISQ = 0.08838834764831845

ENGINES = ("sync", "gpsimd", "vector", "scalar", "tensor")

U13 = 8192
U2 = 4096
EXP_STRIDE = 4 * U13 + 4 * U2
WFF_TOT = NE * EXP_STRIDE


class Sem:
    __slots__ = ("h", "n")

    def __init__(self, h):
        self.h = h
        self.n = 0


def build(nc, dbg=False, lite=False):
    x_h = nc.dram_tensor("x", [NT, D], F32, kind="ExternalInput").ap()
    w1t_h = nc.dram_tensor("w1t", [8, D, H], F32, kind="ExternalInput").ap()
    w3t_h = nc.dram_tensor("w3t", [8, D, H], F32, kind="ExternalInput").ap()
    w2t_h = nc.dram_tensor("w2t", [8, H, D], F32, kind="ExternalInput").ap()
    sw1t_h = nc.dram_tensor("sw1t", [D, H], F32, kind="ExternalInput").ap()
    sw3t_h = nc.dram_tensor("sw3t", [D, H], F32, kind="ExternalInput").ap()
    sw2t_h = nc.dram_tensor("sw2t", [H, D], F32, kind="ExternalInput").ap()
    wq_h = nc.dram_tensor("wq", [D, D], F32, kind="ExternalInput").ap()
    wk_h = nc.dram_tensor("wk", [D, D], F32, kind="ExternalInput").ap()
    wv_h = nc.dram_tensor("wv", [D, D], F32, kind="ExternalInput").ap()
    wo_h = nc.dram_tensor("wo", [D, D], F32, kind="ExternalInput").ap()
    identf_h = nc.dram_tensor("identf", [P, P], F32, kind="ExternalInput").ap()
    identb_h = nc.dram_tensor("identb", [P, P], BF16, kind="ExternalInput").ap()
    sel8_h = nc.dram_tensor("sel8", [8, 8 * P], F32, kind="ExternalInput").ap()
    mask_h = nc.dram_tensor("mask", [P, P], F32, kind="ExternalInput").ap()
    gatei_h = nc.dram_tensor("gatei", [P, DC * 8], F32, kind="ExternalInput").ap()
    bias32_h = nc.dram_tensor("bias32", [P, 32], F32, kind="ExternalInput").ap()
    bq16_h = nc.dram_tensor("bq16", [P, DC], F32, kind="ExternalInput").ap()
    bk16_h = nc.dram_tensor("bk16", [P, DC], F32, kind="ExternalInput").ap()
    lng_h = nc.dram_tensor("lngb", [P, D], F32, kind="ExternalInput").ap()
    lnb_h = nc.dram_tensor("lnbb", [P, D], F32, kind="ExternalInput").ap()
    bvb_h = nc.dram_tensor("bvb", [P, D], F32, kind="ExternalInput").ap()
    bob_h = nc.dram_tensor("bob", [P, D], F32, kind="ExternalInput").ap()
    ones1_h = nc.dram_tensor("ones1", [1, P], F32, kind="ExternalInput").ap()
    onesb_h = nc.dram_tensor("onesb", [P, TT], BF16, kind="ExternalInput").ap()

    y_h = nc.dram_tensor("y", [NT, D], F32, kind="ExternalOutput").ap()

    skind = "ExternalOutput" if dbg else "Internal"
    x1_scr = nc.dram_tensor("x1_scr", [NT, D], F32, kind=skind).ap()
    x2_scr = nc.dram_tensor("x2_scr", [NT, D], F32, kind=skind).ap()
    qt_scr = nc.dram_tensor("qt_scr", [D, NT], BF16, kind=skind).ap()
    kt_scr = nc.dram_tensor("kt_scr", [D, NT], BF16, kind=skind).ap()
    v_scr = nc.dram_tensor("v_scr", [NT, D], BF16, kind=skind).ap()
    avt_scr = nc.dram_tensor("avt_scr", [D, NT], BF16, kind=skind).ap()
    wff_scr = nc.dram_tensor("wff_scr", [P, WFF_TOT], BF16, kind="Internal").ap()
    watt_scr = nc.dram_tensor("watt_scr", [P, 4 * 32768], BF16, kind="Internal").ap()

    items = {e: [] for e in ENGINES}
    chain = {}
    last_link = {}

    def em(eng, fn, waits=(), inc=None, amt=1, nolink=False):
        waits = list(waits)
        if eng in ("vector", "scalar") and amt != 16 and not nolink:
            prev = last_link.get(eng)
            if prev is not None:
                waits.append(prev)
            if inc is None:
                inc = chain[eng]
        if inc is not None:
            inc.n += amt
            if eng in ("vector", "scalar") and amt != 16 and not nolink:
                last_link[eng] = (inc, inc.n)
        items[eng].append((tuple(waits), fn, (inc.h if inc else None), amt))

    with contextlib.ExitStack() as st:
        def sb(name, shape, dt):
            return st.enter_context(nc.sbuf_tensor(name, shape, dt))

        bufA = sb("bufA", [P, 8192], F32)
        bufB = sb("bufB", [P, 8192], F32)
        bufC = sb("bufC", [P, 4096], F32)
        bufD = sb("bufD", [P, 2048], F32)
        bufE = sb("bufE", [P, 1536], F32)
        bufF = sb("bufF", [P, 2048], F32)
        bufW1 = sb("bufW1", [P, 8192], F32)
        bufW2 = sb("bufW2", [P, 8192], F32)
        bufW3 = sb("bufW3", [P, 7168], F32)

        cs_identf = sb("cs_identf", [P, P], F32)
        cs_identb = sb("cs_identb", [P, P], BF16)
        cs_sel8 = sb("cs_sel8", [8, 8 * P], F32)
        cs_mask = sb("cs_mask", [P, P], F32)
        cs_gatei = sb("cs_gatei", [P, DC * 8], F32)
        cs_bias32 = sb("cs_bias32", [P, 32], F32)
        cs_bq = sb("cs_bq", [P, DC], F32)
        cs_bk = sb("cs_bk", [P, DC], F32)
        cs_ones1 = sb("cs_ones1", [1, P], F32)
        cs_onesb = sb("cs_onesb", [P, TT], BF16)
        cs_eps = sb("cs_eps", [P, 1], F32)

        ps_big = st.enter_context(nc.psum_tensor("ps_big", [P, 2048], F32))[:]
        psE = st.enter_context(nc.psum_tensor("psE", [P, 512], F32))[:]
        psF = st.enter_context(nc.psum_tensor("psF", [P, 512], F32))[:]
        psG = st.enter_context(nc.psum_tensor("psG", [P, 1024], BF16))[:]
        psH = st.enter_context(nc.psum_tensor("psH", [P, 512], F32))[:]
        psGf = psG.bitcast(F32)
        psA = ps_big[:, 0:512]
        psB = ps_big[:, 512:1024]
        psC = ps_big[:, 1024:1536]
        psD = ps_big[:, 1536:2048]

        def sem(name):
            return Sem(st.enter_context(nc.semaphore(name)))

        chain["vector"] = sem("s_chv")
        chain["scalar"] = sem("s_chs")
        s_ld = sem("s_ld")
        s_x = sem("s_x")
        s_stx = sem("s_stx")
        s_sta = sem("s_sta")
        s_wlp = [sem("s_wlA"), sem("s_wlB"), sem("s_wlC")]
        s_w2lp = [sem("s_w2lA"), sem("s_w2lB")]
        s_wlc = [sem("s_wlcA"), sem("s_wlcB"), sem("s_wlcC")]
        s_w2c = [sem("s_w2cA"), sem("s_w2cB")]
        s_wst13 = [sem("s_wst13A"), sem("s_wst13B"), sem("s_wst13C")]
        s_wst2 = [sem("s_wst2A"), sem("s_wst2B")]
        s_stq = [sem("s_stqA"), sem("s_stqB")]
        s_stv = [sem("s_stvA"), sem("s_stvB")]
        s_qkp = [sem("s_qkA"), sem("s_qkB")]
        s_qkw = [sem("s_qkwA"), sem("s_qkwB"), sem("s_qkwC"), sem("s_qkwD")]
        s_vw = [sem("s_vwA"), sem("s_vwB")]
        s_qkl = [sem("s_qklA"), sem("s_qklB"), sem("s_qklC"), sem("s_qklD")]
        s_vl = [sem("s_vlA"), sem("s_vlB")]
        s_ast = [sem("s_astA"), sem("s_astB"), sem("s_astC"), sem("s_astD")]
        s_vst = [sem("s_vstA"), sem("s_vstB")]
        s_ab = sem("s_ab")
        s_evA = sem("s_evA")
        s_h = sem("s_h")
        s_w2p = sem("s_w2p")
        s_accd = sem("s_accd")
        s_ep = sem("s_ep")
        s_rt = sem("s_rt")
        s_tp = sem("s_tp")
        s_tpe = sem("s_tpe")
        s_gb = sem("s_gb")
        s_xte = sem("s_xte")
        s_pb = sem("s_pb")
        s_sc = sem("s_sc")
        s_av = sem("s_av")

        def barrier():
            for e in ENGINES:
                em(e, lambda en: en.nop(), inc=s_pb, nolink=True)
            tgt = s_pb.n
            for e in ENGINES:
                em(e, lambda en: en.nop(), waits=[(s_pb, tgt)], nolink=True)

        for dst, src in ((cs_identf, identf_h), (cs_identb, identb_h),
                         (cs_sel8, sel8_h), (cs_mask, mask_h),
                         (cs_gatei, gatei_h), (cs_bias32, bias32_h),
                         (cs_bq, bq16_h), (cs_bk, bk16_h),
                         (cs_ones1, ones1_h), (cs_onesb, onesb_h)):
            em("sync", lambda e, d=dst, s=src: e.dma_start(out=d[:], in_=s[:]),
               inc=s_ld, amt=16)
        em("vector", lambda e: e.memset(cs_eps[:], EPS), inc=s_ld)
        ld0 = s_ld.n
        for e in ENGINES:
            em(e, lambda en: en.nop(), waits=[(s_ld, ld0)])

        xnT = bufC[:].bitcast(BF16)
        xsT = [bufW1[:, 0:4096].bitcast(BF16)]
        w13 = [bufW2[:, 0:4096].bitcast(BF16),
               bufW2[:, 4096:8192].bitcast(BF16),
               bufW1[:, 4096:8192].bitcast(BF16)]
        w2q = [bufW3[:, 0:2048].bitcast(BF16),
               bufW3[:, 2048:4096].bitcast(BF16)]
        ha2 = [bufW3[:, 4096:4352].bitcast(BF16),
               bufW3[:, 4352:4608].bitcast(BF16)]
        h_sb = bufW3[:, 5120:7168].bitcast(BF16)
        gb_all = bufF[:].bitcast(BF16)
        sq_ffn = bufW1[:, 0:2048]
        sq_a2 = bufB[:, 6144:8192]

        mean_c = bufE[:, 0:1]
        var_c = bufE[:, 1:2]
        std_c = bufE[:, 2:3]
        rstd_c = bufE[:, 3:4]
        mean2_c = bufE[:, 4:5]
        var2_c = bufE[:, 5:6]
        std2_c = bufE[:, 6:7]
        rstd2_c = bufE[:, 7:8]
        nmax_c = bufE[:, 8:9]
        lsum_c = bufE[:, 9:10]
        rinv_c = bufE[:, 10:11]
        bnst = bufE[:, 24:48]
        rt_tmp = bufE[:, 64:576]
        t8_sb = bufE[:, 640:648]
        m1_c = bufE[:, 648:649]
        m2_c = bufE[:, 649:650]
        gT_sb = bufE[0:8, 704:1216]
        rrow_sb = bufE[0:1, 1216:1344]
        rbc_sb = bufE[:, 1344:1472]

        def s_col(g):
            return bufE[:, 576 + g * 8:576 + (g + 1) * 8]

        def sp_col(g):
            return bufE[:, 608 + g * 8:608 + (g + 1) * 8]

        def g_col(g):
            return bufE[:, 656 + g * 8:656 + (g + 1) * 8]

        gs = dict(k=0, j=0, gi=0, r=0, k_ld=[], j_ld=[], e_ab=[],
                  evA=[], hml=[], tp_evac=None, wst13=[], wst2=[],
                  wst_done=False)

        def ln_plain(xg, out_ap, sq_ap, sq_guard):
            em("vector", lambda e: e.reduce_sum(mean_c, xg, axis=AX.X))
            em("vector", lambda e: e.tensor_scalar_mul(mean_c, mean_c, INV_D))
            em("vector", lambda e: e.tensor_scalar(
                out=out_ap, in0=xg, scalar1=mean_c, scalar2=None,
                op0=ALU.subtract), inc=s_rt)
            em("scalar", lambda e: e.square(sq_ap, out_ap),
               waits=[(s_rt, s_rt.n)] + sq_guard, inc=s_rt)
            em("vector", lambda e: e.reduce_sum(var_c, sq_ap, axis=AX.X),
               waits=[(s_rt, s_rt.n)])
            em("vector", lambda e: e.tensor_scalar_mul(var_c, var_c, INV_D),
               inc=s_rt)
            em("scalar", lambda e: e.activation(
                std_c, var_c, AF.Sqrt, bias=cs_eps[:], scale=1.0),
               waits=[(s_rt, s_rt.n)], inc=s_rt)
            em("vector", lambda e: e.reciprocal(rstd_c, std_c),
               waits=[(s_rt, s_rt.n)])
            em("vector", lambda e: e.tensor_scalar(
                out=out_ap, in0=xg, scalar1=mean_c, scalar2=rstd_c,
                op0=ALU.subtract, op1=ALU.mult), inc=s_rt)

        def transpose_group(g, src2048, dst_bf, extra_f32=None, router_g=None,
                            evac_wait=None, router_wait=None):
            xn_done = s_rt.n
            for b in range(4):
                wts = [(s_rt, xn_done)] if b == 0 else []
                if gs["tp_evac"] is not None:
                    wts = wts + [(s_tpe, gs["tp_evac"])]
                for i in range(4):
                    c = b * 4 + i
                    em("tensor", lambda e, c=c, i=i, src2048=src2048: e.transpose(
                        psGf[:, i * P:(i + 1) * P],
                        src2048[:, c * P:(c + 1) * P], cs_identf[:]),
                       waits=wts if i == 0 else [],
                       inc=s_tp if i == 3 else None)
                tpv = s_tp.n
                ewt = [(s_tp, tpv)]
                if b == 0 and g == 0 and evac_wait is not None:
                    ewt.append(evac_wait)
                em("scalar", lambda e, b=b, g=g, dst_bf=dst_bf: e.copy(
                    dst_bf.rearrange("p (c n) -> p c n", c=DC)[
                        :, b * 4:(b + 1) * 4, g * P:(g + 1) * P],
                    psGf[:].rearrange("p (c n) -> p c n", c=4)),
                   waits=ewt,
                   inc=None if router_g is not None else s_tpe)
                if router_g is not None:
                    em("scalar", lambda e: e.copy(rt_tmp, psGf[:]),
                       inc=s_tpe)
                    evv = s_tpe.n
                    for i in range(4):
                        cc = b * 4 + i
                        rwt = [(s_tpe, evv)] if i == 0 else []
                        if b == 0 and i == 0 and g == 0 and router_wait is not None:
                            rwt.append(router_wait)
                        em("tensor", lambda e, i=i, cc=cc, g=g, b=b: e.matmul(
                            psH[:, g * 8:(g + 1) * 8],
                            rt_tmp[:, i * P:(i + 1) * P],
                            cs_gatei[:, cc * 8:(cc + 1) * 8],
                            start=(b == 0 and i == 0),
                            stop=(b == 3 and i == 3)),
                           waits=rwt,
                           inc=s_rt if (b == 3 and i == 3) else None)
                gs["tp_evac"] = s_tpe.n

        def ffn_pass(src_ap, dst_ap, precast):
            for t in range(2 if lite else NTILE):
                if gs.get("prol_for") != (id(src_ap), t):
                    wts = []
                    if gs.get("sh_acc"):
                        wts.append((s_accd, gs["sh_acc"]))
                    if t == 0 and s_stx.n:
                        wts.append((s_stx, s_stx.n))
                    em("sync", lambda e, t=t, src=src_ap: e.dma_start(
                        out=bufA[:].rearrange("p (g f) -> p g f", g=NG),
                        in_=src[t * TT:(t + 1) * TT, :].rearrange(
                            "(g p) f -> p g f", p=P)),
                       waits=wts, inc=s_x, amt=16)
                x_ready = s_x.n

                first_nonpre = (not gs["wst_done"]) and not (precast and t == 0) \
                    and len(gs["wst13"]) > 0
                accd_base = s_accd.n
                ntile_eff2 = 2 if lite else NTILE
                for e9 in ([8] + list(range(8))):
                    for q in range(4):
                        k = gs["k"]
                        gs["k"] += 1
                        par = k % 3
                        off = e9 * EXP_STRIDE + q * U13
                        wts = []
                        if k >= 3:
                            wts.append((s_ab, 4 * (k - 2)))
                        if precast and t == 0:
                            if k >= 3:
                                wts.append((s_wst13[par], gs["wst13"][k - 3]))
                            s1 = w1t_h[e9] if e9 < 8 else sw1t_h
                            s3 = w3t_h[e9] if e9 < 8 else sw3t_h
                            em("gpsimd", lambda e, par=par, s1=s1, q=q: e.dma_start(
                                out=w13[par][:, 0:4096].rearrange(
                                    "p (c n) -> p c n", c=DC),
                                in_=s1.rearrange("(c p) n -> p c n", p=P)[
                                    :, :, q * QH:(q + 1) * QH]),
                               waits=wts, inc=s_wlc[par], amt=16)
                            em("gpsimd", lambda e, par=par, s3=s3, q=q: e.dma_start(
                                out=w13[par][:, 4096:8192].rearrange(
                                    "p (c n) -> p c n", c=DC),
                                in_=s3.rearrange("(c p) n -> p c n", p=P)[
                                    :, :, q * QH:(q + 1) * QH]),
                               inc=s_wlc[par], amt=16)
                            em("sync", lambda e, par=par, off=off: e.dma_start(
                                out=wff_scr[:, off:off + U13], in_=w13[par][:]),
                               waits=[(s_wlc[par], s_wlc[par].n)],
                               inc=s_wst13[par], amt=16)
                            gs["wst13"].append(s_wst13[par].n)
                            gs["k_ld"].append((s_wlc[par], s_wlc[par].n))
                        else:
                            if first_nonpre:
                                wts += [(s, s.n) for s in s_wst13]
                                wts += [(s, s.n) for s in s_wst2]
                                first_nonpre = False
                                gs["wst_done"] = True
                            em("sync", lambda e, par=par, off=off: e.dma_start(
                                out=w13[par][:], in_=wff_scr[:, off:off + U13]),
                               waits=wts, inc=s_wlp[par], amt=16)
                            gs["k_ld"].append((s_wlp[par], s_wlp[par].n))
                    for q in range(4):
                        j = gs["j"]
                        gs["j"] += 1
                        par = j % 2
                        off = e9 * EXP_STRIDE + 4 * U13 + q * U2
                        wts = []
                        if j >= 2:
                            wts.append((s_accd, 4 * (j - 1)))
                        if precast and t == 0:
                            if j >= 2:
                                wts.append((s_wst2[par], gs["wst2"][j - 2]))
                            s2 = w2t_h[e9] if e9 < 8 else sw2t_h
                            em("gpsimd", lambda e, par=par, s2=s2, q=q: e.dma_start(
                                out=w2q[par][:].rearrange(
                                    "p (c n) -> p c n", c=HC),
                                in_=s2.rearrange("(c p) n -> p c n", p=P)[
                                    :, :, q * 512:(q + 1) * 512]),
                               waits=wts, inc=s_w2c[par], amt=16)
                            em("sync", lambda e, par=par, off=off: e.dma_start(
                                out=wff_scr[:, off:off + U2], in_=w2q[par][:]),
                               waits=[(s_w2c[par], s_w2c[par].n)],
                               inc=s_wst2[par], amt=16)
                            gs["wst2"].append(s_wst2[par].n)
                            gs["j_ld"].append((s_w2c[par], s_w2c[par].n))
                        else:
                            em("sync", lambda e, par=par, off=off: e.dma_start(
                                out=w2q[par][:], in_=wff_scr[:, off:off + U2]),
                               waits=wts, inc=s_w2lp[par], amt=16)
                            gs["j_ld"].append((s_w2lp[par], s_w2lp[par].n))
                    if e9 == 0 and t + 1 < ntile_eff2:
                        em("sync", lambda e, t=t, src=src_ap: e.dma_start(
                            out=bufA[:].rearrange("p (g f) -> p g f", g=NG),
                            in_=src[(t + 1) * TT:(t + 2) * TT, :].rearrange(
                                "(g p) f -> p g f", p=P)),
                           waits=[(s_accd, accd_base + 16)], inc=s_x, amt=16)
                        gs["x_next"] = s_x.n

                def prologue(x_ready_v, hoisted):
                    ew = (s_xte, gs["xte_v"][8]) if (hoisted and 8 in gs["xte_v"]) else None
                    rw = (s_gb, s_gb.n) if hoisted else None
                    for g in range(NG):
                        xg = bufA[:, g * D:(g + 1) * D]
                        w0 = [(s_x, x_ready_v)] if g == 0 else []
                        if g > 0 or hoisted:
                            w0.append((s_tp, s_tp.n))
                        for c in range(4):
                            em("vector", lambda e, xg=xg, c=c: e.bn_stats(
                                bnst[:, c * 6:(c + 1) * 6],
                                xg[:, c * 512:(c + 1) * 512]),
                               waits=w0 if c == 0 else [])
                        em("vector", lambda e: e.bn_aggr(bufE[:, 0:2], bnst))
                        em("scalar", lambda e: e.activation(
                            std_c, var_c, AF.Sqrt, bias=cs_eps[:], scale=1.0),
                           waits=[(chain["vector"], chain["vector"].n)],
                           inc=s_rt)
                        em("vector", lambda e: e.reciprocal(rstd_c, std_c),
                           waits=[(s_rt, s_rt.n)])
                        em("vector", lambda e, xg=xg: e.tensor_scalar(
                            out=bufD[:], in0=xg, scalar1=mean_c,
                            scalar2=rstd_c,
                            op0=ALU.subtract, op1=ALU.mult), inc=s_rt)
                        transpose_group(g, bufD[:], xnT, router_g=g,
                                        evac_wait=ew if g == 0 else None,
                                        router_wait=rw if g == 0 else None)
                    gs["xnt_ready"] = gs["tp_evac"]

                if gs.get("prol_for") != (id(src_ap), t):
                    prologue(x_ready, False)
                xnt_ready = gs["xnt_ready"]

                def scale_block(idx, e9, scale_wts):
                    par = 0
                    wts = list(scale_wts)
                    if idx >= 1 and len(gs["e_ab"]) >= 1:
                        wts.append((s_ab, gs["e_ab"][-1]))
                    gsrc = cs_onesb[:] if e9 == 8 else \
                        gb_all[:, e9 * TT:(e9 + 1) * TT]
                    for c in range(DC):
                        em("vector", lambda e, c=c, par=par, gsrc=gsrc:
                           e.tensor_tensor(
                               xsT[par][:, c * TT:(c + 1) * TT],
                               xnT[:, c * TT:(c + 1) * TT], gsrc, op=ALU.mult),
                           waits=wts if c == 0 else [],
                           inc=s_xte if c == DC - 1 else None)
                    gs["xte_v"][idx] = s_xte.n

                def expert_block(idx, e9, nxt=None):
                    par = 0
                    xte_done = gs["xte_v"][idx]
                    for q in range(4):
                        kk = len(gs["k_ld"]) - (NE - idx) * 4 + q
                        ld_need = gs["k_ld"][kk]
                        wpar = kk % 3
                        for hh in range(2):
                            gi = gs["gi"]
                            gs["gi"] += 1
                            a_bank = (psA, psC)[gi % 2]
                            b_bank = (psB, psD)[gi % 2]
                            wts = []
                            if q == 0 and hh == 0:
                                wts.append((s_xte, xte_done))
                            if hh == 0:
                                wts.append(ld_need)
                            if len(gs["evA"]) >= 2:
                                wts.append((s_evA, gs["evA"][-2]))
                            for c in range(DC):
                                em("tensor", lambda e, c=c, hh=hh, wpar=wpar,
                                   par=par, a_bank=a_bank: e.matmul(
                                       a_bank,
                                       w13[wpar][:, c * QH + hh * P:
                                                 c * QH + (hh + 1) * P],
                                       xsT[par][:, c * TT:(c + 1) * TT],
                                       start=(c == 0), stop=(c == DC - 1)),
                                   waits=wts if c == 0 else [],
                                   inc=s_ab if c == DC - 1 else None)
                            wts2 = []
                            if len(gs["hml"]) >= 2:
                                wts2.append((s_h, gs["hml"][-2]))
                            for c in range(DC):
                                em("tensor", lambda e, c=c, hh=hh, wpar=wpar,
                                   par=par, b_bank=b_bank: e.matmul(
                                       b_bank,
                                       w13[wpar][:, 4096 + c * QH + hh * P:
                                                 4096 + c * QH + (hh + 1) * P],
                                       xsT[par][:, c * TT:(c + 1) * TT],
                                       start=(c == 0), stop=(c == DC - 1)),
                                   waits=wts2 if c == 0 else [],
                                   inc=s_ab if c == DC - 1 else None)
                            ab_now = s_ab.n
                            silu_w = [(s_ab, ab_now - 1)]
                            if len(gs["hml"]) >= 2:
                                silu_w.append((s_h, gs["hml"][-2]))
                            em("scalar", lambda e, gi=gi, a_bank=a_bank:
                               e.activation(ha2[gi % 2][:], a_bank, AF.Silu),
                               waits=silu_w, inc=s_evA)
                            gs["evA"].append(s_evA.n)
                            hc_i = q * 2 + hh
                            em("vector", lambda e, gi=gi, hc_i=hc_i,
                               b_bank=b_bank: e.tensor_tensor(
                                   h_sb[:, hc_i * TT:(hc_i + 1) * TT],
                                   b_bank, ha2[gi % 2][:], op=ALU.mult),
                               waits=[(s_ab, ab_now), (s_evA, s_evA.n)],
                               inc=s_h)
                            gs["hml"].append(s_h.n)
                    gs["e_ab"].append(s_ab.n)
                    h_all = s_h.n
                    if nxt is not None:
                        scale_block(nxt[0], nxt[1], [(s_gb, gb_done)])

                    for q2 in range(4):
                        jj = len(gs["j_ld"]) - (NE - idx) * 4 + q2
                        ld_need = gs["j_ld"][jj]
                        wpar = jj % 2
                        for tg in range(NG):
                            r = gs["r"]
                            gs["r"] += 1
                            bank = (psE, psF)[r % 2]
                            wts = []
                            if q2 == 0 and tg == 0:
                                wts.append((s_h, h_all))
                            if tg == 0:
                                wts.append(ld_need)
                            if r >= 2:
                                wts.append((s_accd, r - 1))
                            for c in range(HC):
                                em("tensor", lambda e, c=c, tg=tg, wpar=wpar,
                                   bank=bank: e.matmul(
                                       bank,
                                       h_sb.rearrange("p (c n) -> p c n", c=HC)[
                                           :, c, tg * P:(tg + 1) * P],
                                       w2q[wpar][:, c * 512:(c + 1) * 512],
                                       start=(c == 0), stop=(c == HC - 1)),
                                   waits=wts if c == 0 else [],
                                   inc=s_w2p if c == HC - 1 else None)
                            accsl = bufB[:, tg * D + q2 * 512:
                                         tg * D + (q2 + 1) * 512]
                            if idx == 0:
                                w0 = [(s_w2p, s_w2p.n)]
                                if q2 == 0 and tg == 0 and gs.get("st_prev"):
                                    w0.append((s_stx, gs["st_prev"]))
                                xsl = bufA[:, tg * D + q2 * 512:
                                           tg * D + (q2 + 1) * 512]
                                em("vector", lambda e, accsl=accsl, bank=bank,
                                   xsl=xsl: e.scalar_tensor_tensor(
                                       out=accsl, in0=xsl, scalar=2.0,
                                       in1=bank, op0=ALU.mult, op1=ALU.add),
                                   waits=w0, inc=s_accd)
                            else:
                                em("vector", lambda e, accsl=accsl, bank=bank:
                                   e.tensor_tensor(accsl, accsl, bank,
                                                   op=ALU.add),
                                   waits=[(s_w2p, s_w2p.n)], inc=s_accd)

                gs["xte_v"] = {}
                scale_block(0, 8, [(s_tpe, xnt_ready)])
                expert_block(0, 8)
                gs["sh_acc"] = s_accd.n

                em("scalar", lambda e: e.activation(
                    bufE[:, 576:608], psH[:, 0:32], AF.Sigmoid),
                   waits=[(s_rt, s_rt.n)], inc=s_rt)
                em("vector", lambda e: e.tensor_add(
                    bufE[:, 608:640], bufE[:, 576:608], cs_bias32[:]),
                   waits=[(s_rt, s_rt.n)])
                for g in range(NG):
                    em("vector", lambda e, g=g: e.reduce_max(
                        m1_c, sp_col(g), axis=AX.X))
                    em("vector", lambda e, g=g: e.tensor_single_scalar(
                        out=t8_sb, in_=sp_col(g), scalar=m1_c, op=ALU.is_ge))
                    em("vector", lambda e, g=g: e.scalar_tensor_tensor(
                        out=t8_sb, in0=t8_sb, scalar=-1e30, in1=sp_col(g),
                        op0=ALU.mult, op1=ALU.add))
                    em("vector", lambda e: e.reduce_max(m2_c, t8_sb, axis=AX.X))
                    em("vector", lambda e, g=g: e.tensor_single_scalar(
                        out=t8_sb, in_=sp_col(g), scalar=m2_c, op=ALU.is_ge))
                    em("vector", lambda e, g=g: e.tensor_mul(
                        g_col(g), s_col(g), t8_sb), inc=s_rt)
                gates_done = s_rt.n
                for g in range(NG):
                    em("tensor", lambda e, g=g: e.transpose(
                        psH[0:8, g * P:(g + 1) * P], g_col(g), cs_identf[:]),
                       waits=[(s_rt, gates_done)] if g == 0 else [],
                       inc=s_tp if g == 3 else None)
                em("scalar", lambda e: e.copy(gT_sb, psH[0:8, :]),
                   waits=[(s_tp, s_tp.n)], inc=s_gb)
                for e8 in range(8):
                    em("tensor", lambda e, e8=e8: e.matmul(
                        psH[:], cs_sel8[:, e8 * P:(e8 + 1) * P], gT_sb,
                        start=True, stop=True),
                       waits=[(s_gb, s_gb.n)], inc=s_tp)
                    em("scalar", lambda e, e8=e8: e.copy(
                        gb_all[:, e8 * TT:(e8 + 1) * TT], psH[:]),
                       waits=[(s_tp, s_tp.n)], inc=s_gb)
                gb_done = s_gb.n

                scale_block(1, 0, [(s_gb, gb_done)])
                ntile_eff = 2 if lite else NTILE
                for e8 in range(8):
                    expert_block(e8 + 1, e8,
                                 nxt=(e8 + 2, e8 + 1) if e8 < 7 else None)
                    if e8 == 6 and t + 1 < ntile_eff:
                        prologue(gs["x_next"], True)
                        gs["prol_for"] = (id(src_ap), t + 1)

                acc_all = s_accd.n
                for tg in range(NG):
                    for q2 in range(4):
                        sl = slice(tg * D + q2 * 512, tg * D + (q2 + 1) * 512)
                        em("vector", lambda e, sl=sl: e.tensor_scalar_mul(
                            bufB[:, sl], bufB[:, sl], 0.5),
                           waits=[(s_accd, acc_all)] if (tg == 0 and q2 == 0)
                           else [],
                           inc=s_ep if (tg == 3 and q2 == 3) else None)
                em("scalar", lambda e, t=t, dst=dst_ap: e.dma_start(
                    out=dst[t * TT:(t + 1) * TT, :].rearrange(
                        "(g p) f -> p g f", p=P),
                    in_=bufB[:].rearrange("p (g f) -> p g f", g=NG)),
                   waits=[(s_ep, s_ep.n)], inc=s_stx, amt=16)
                gs["st_prev"] = s_stx.n

        def qkv_pass():
            lngb = bufB[:, 0:2048]
            lnbb = bufB[:, 2048:4096]
            bvb = bufB[:, 4096:6144]
            for dst, src in ((lngb, lng_h), (lnbb, lnb_h), (bvb, bvb_h)):
                em("sync", lambda e, d=dst, s=src: e.dma_start(out=d, in_=s[:]),
                   waits=[(s_stx, s_stx.n)], inc=s_ld, amt=16)
            ldv = s_ld.n
            qkw = [bufW2[:, 0:1024].bitcast(BF16),
                   bufW2[:, 1024:2048].bitcast(BF16),
                   bufW2[:, 2048:3072].bitcast(BF16),
                   bufW2[:, 3072:4096].bitcast(BF16)]
            vw = [bufW1[:, 0:4096].bitcast(BF16),
                  bufW1[:, 4096:8192].bitcast(BF16)]
            qstage = [bufE[:, 64:320].bitcast(BF16),
                      bufE[:, 320:576].bitcast(BF16)]
            vstage = [bufE[:, 576:832].bitcast(BF16),
                      bufE[:, 832:1088].bitcast(BF16)]
            XnT = bufC[:].bitcast(BF16)
            a2 = dict(qk_u=0, v_u=0, qk_done=[], qk_ev=[], qk_st=[],
                      v_done=[], v_ev=[], v_st=[], qk_stw=[], v_stw=[],
                      qk_u0=32)

            for t in range(2 if lite else NTILE):
                wtsx = [(s_ld, ldv), (s_stx, s_stx.n)]
                if a2.get("prol"):
                    wtsx.append((s_rt, a2["prol"]))
                em("scalar", lambda e, t=t: e.dma_start(
                    out=bufA[:].rearrange("p (g f) -> p g f", g=NG),
                    in_=x1_scr[t * TT:(t + 1) * TT, :].rearrange(
                        "(g p) f -> p g f", p=P)),
                   waits=wtsx, inc=s_x, amt=16)
                x_ready = s_x.n
                for g in range(NG):
                    xg = bufA[:, g * D:(g + 1) * D]
                    w0 = [(s_x, x_ready)] if g == 0 else []
                    if g > 0:
                        w0.append((s_tp, s_tp.n))
                    for c in range(4):
                        em("vector", lambda e, xg=xg, c=c: e.bn_stats(
                            bnst[:, c * 6:(c + 1) * 6],
                            xg[:, c * 512:(c + 1) * 512]),
                           waits=w0 if c == 0 else [])
                    em("vector", lambda e: e.bn_aggr(bufE[:, 0:2], bnst))
                    em("scalar", lambda e: e.activation(
                        std_c, var_c, AF.Sqrt, bias=cs_eps[:], scale=1.0),
                       waits=[(chain["vector"], chain["vector"].n)], inc=s_rt)
                    em("vector", lambda e: e.reciprocal(rstd_c, std_c),
                       waits=[(s_rt, s_rt.n)])
                    em("vector", lambda e, xg=xg: e.tensor_scalar(
                        out=xg, in0=xg, scalar1=mean_c, scalar2=rstd_c,
                        op0=ALU.subtract, op1=ALU.mult))
                    for c in range(4):
                        em("vector", lambda e, xg=xg, c=c: e.bn_stats(
                            bnst[:, c * 6:(c + 1) * 6],
                            xg[:, c * 512:(c + 1) * 512]))
                    em("vector", lambda e: e.bn_aggr(bufE[:, 4:6], bnst))
                    em("scalar", lambda e: e.activation(
                        std2_c, var2_c, AF.Sqrt, bias=cs_eps[:], scale=1.0),
                       waits=[(chain["vector"], chain["vector"].n)], inc=s_rt)
                    em("vector", lambda e: e.reciprocal(rstd2_c, std2_c),
                       waits=[(s_rt, s_rt.n)])
                    em("vector", lambda e, xg=xg: e.tensor_scalar(
                        out=bufD[:], in0=xg, scalar1=mean2_c, scalar2=rstd2_c,
                        op0=ALU.subtract, op1=ALU.mult))
                    em("vector", lambda e: e.tensor_mul(bufD[:], bufD[:], lngb))
                    em("vector", lambda e: e.tensor_add(bufD[:], bufD[:], lnbb),
                       inc=s_rt)
                    transpose_group(g, bufD[:], XnT)
                a2["prol"] = s_rt.n
                xnt_done = s_tpe.n

                for wi, (w_src, scr, bias_t) in enumerate(
                        ((wq_h, qt_scr, cs_bq), (wk_h, kt_scr, cs_bk))):
                    for dc in range(DC):
                        u = a2["qk_u"]
                        a2["qk_u"] += 1
                        wpar = u % 4
                        woff = wi * 32768 + dc * 2048
                        wts = []
                        if len(a2["qk_done"]) >= 4:
                            wts.append((s_tp, a2["qk_done"][-4]))
                        if t == 0:
                            if len(a2["qk_stw"]) >= 4:
                                wts.append(a2["qk_stw"][-4])
                            em("gpsimd", lambda e, wpar=wpar, dc=dc, ws=w_src:
                               e.dma_start(
                                   out=qkw[wpar][:].rearrange(
                                       "p (c n) -> p c n", c=DC),
                                   in_=ws.rearrange("(c p) n -> p c n", p=P)[
                                       :, :, dc * P:(dc + 1) * P]),
                               waits=wts, inc=s_qkw[wpar], amt=16)
                            ld_rec = (s_qkw[wpar], s_qkw[wpar].n)
                            em("sync", lambda e, wpar=wpar, woff=woff:
                               e.dma_start(
                                   out=watt_scr[:, woff:woff + 2048],
                                   in_=qkw[wpar][:]),
                               waits=[ld_rec], inc=s_ast[wpar], amt=16)
                            a2["qk_stw"].append((s_ast[wpar], s_ast[wpar].n))
                        else:
                            if t == 1 and u == a2["qk_u0"]:
                                wts += [(s, s.n) for s in s_ast]
                            em("sync", lambda e, wpar=wpar, woff=woff:
                               e.dma_start(
                                   out=qkw[wpar][:],
                                   in_=watt_scr[:, woff:woff + 2048]),
                               waits=wts, inc=s_qkl[wpar], amt=16)
                            ld_rec = (s_qkl[wpar], s_qkl[wpar].n)
                        bank = (psA, psB)[u % 2]
                        wts2 = [ld_rec, (s_tpe, xnt_done)]
                        if len(a2["qk_ev"]) >= 2:
                            wts2.append((s_sc, a2["qk_ev"][-2]))
                        for c in range(DC):
                            em("tensor", lambda e, c=c, wpar=wpar, bank=bank,
                               XnT=XnT: e.matmul(
                                   bank, qkw[wpar][:, c * P:(c + 1) * P],
                                   XnT[:, c * TT:(c + 1) * TT],
                                   start=(c == 0), stop=(c == DC - 1)),
                               waits=wts2 if c == 0 else [],
                               inc=s_tp if c == DC - 1 else None)
                        a2["qk_done"].append(s_tp.n)
                        sp = u % 2
                        wts3 = [(s_tp, s_tp.n)]
                        if len(a2["qk_st"]) >= 2:
                            wts3.append((s_stq[sp], a2["qk_st"][-2]))
                        qsc = ISQ if wi == 0 else 1.0
                        em("scalar", lambda e, sp=sp, dc=dc, bank=bank,
                           bias_t=bias_t, qsc=qsc: e.activation(
                               qstage[sp][:], bank, AF.Identity,
                               bias=bias_t[:, dc:dc + 1], scale=qsc),
                           waits=wts3, inc=s_sc)
                        a2["qk_ev"].append(s_sc.n)
                        em("scalar", lambda e, sp=sp, dc=dc, t=t, scr=scr:
                           e.dma_start(
                               out=scr[dc * P:(dc + 1) * P,
                                       t * TT:(t + 1) * TT],
                               in_=qstage[sp][:]),
                           waits=[(s_sc, s_sc.n)], inc=s_stq[sp], amt=16)
                        a2["qk_st"].append(s_stq[sp].n)

                for dg in range(4):
                    u = a2["v_u"]
                    a2["v_u"] += 1
                    wpar = u % 2
                    woff = 2 * 32768 + dg * 8192
                    wts = []
                    if len(a2["v_done"]) >= 2:
                        wts.append((s_tp, a2["v_done"][-2]))
                    if t == 0:
                        if len(a2["v_stw"]) >= 2:
                            wts.append(a2["v_stw"][-2])
                        em("gpsimd", lambda e, wpar=wpar, dg=dg: e.dma_start(
                            out=vw[wpar][:].rearrange("p (c n) -> p c n", c=DC),
                            in_=wv_h.rearrange("(c p) n -> p c n", p=P)[
                                :, :, dg * 512:(dg + 1) * 512]),
                           waits=wts, inc=s_vw[wpar], amt=16)
                        ld_rec = (s_vw[wpar], s_vw[wpar].n)
                        em("sync", lambda e, wpar=wpar, woff=woff: e.dma_start(
                            out=watt_scr[:, woff:woff + 8192], in_=vw[wpar][:]),
                           waits=[ld_rec], inc=s_vst[wpar], amt=16)
                        a2["v_stw"].append((s_vst[wpar], s_vst[wpar].n))
                    else:
                        if t == 1 and dg == 0:
                            wts += [(s_vst[0], s_vst[0].n),
                                    (s_vst[1], s_vst[1].n)]
                        em("sync", lambda e, wpar=wpar, woff=woff: e.dma_start(
                            out=vw[wpar][:], in_=watt_scr[:, woff:woff + 8192]),
                           waits=wts, inc=s_vl[wpar], amt=16)
                        ld_rec = (s_vl[wpar], s_vl[wpar].n)
                    unit_last = None
                    for tg in range(NG):
                        bank = (psC, psD)[(u * NG + tg) % 2]
                        wts2 = [ld_rec, (s_tpe, xnt_done)] if tg == 0 else []
                        if len(a2["v_ev"]) >= 2:
                            wts2.append((s_av, a2["v_ev"][-2]))
                        for c in range(DC):
                            em("tensor", lambda e, c=c, tg=tg, wpar=wpar,
                               bank=bank, XnT=XnT: e.matmul(
                                   bank,
                                   XnT.rearrange("p (c n) -> p c n", c=DC)[
                                       :, c, tg * P:(tg + 1) * P],
                                   vw[wpar][:, c * 512:(c + 1) * 512],
                                   start=(c == 0), stop=(c == DC - 1)),
                               waits=wts2 if c == 0 else [],
                               inc=s_tp if c == DC - 1 else None)
                        unit_last = s_tp.n
                        sp = (u * NG + tg) % 2
                        wts3 = [(s_tp, s_tp.n)]
                        if len(a2["v_st"]) >= 2:
                            wts3.append((s_stv[sp], a2["v_st"][-2]))
                        em("vector", lambda e, sp=sp, dg=dg, bank=bank:
                           e.tensor_tensor(
                               vstage[sp][:], bank,
                               bvb[:, dg * 512:(dg + 1) * 512], op=ALU.add),
                           waits=wts3, inc=s_av)
                        a2["v_ev"].append(s_av.n)
                        em("scalar", lambda e, sp=sp, dg=dg, tg=tg, t=t:
                           e.dma_start(
                               out=v_scr[t * TT + tg * P:
                                         t * TT + (tg + 1) * P,
                                         dg * 512:(dg + 1) * 512],
                               in_=vstage[sp][:]),
                           waits=[(s_av, s_av.n)], inc=s_stv[sp], amt=16)
                        a2["v_st"].append(s_stv[sp].n)
                    a2["v_done"].append(unit_last)

        def attn_pass():
            qth = [bufW1[:, 0:1024].bitcast(BF16),
                   bufW1[:, 3072:4096].bitcast(BF16)]
            kth = [bufW1[:, 1024:2048].bitcast(BF16),
                   bufW1[:, 4096:5120].bitcast(BF16)]
            vth = [bufW1[:, 2048:3072].bitcast(BF16),
                   bufW1[:, 5120:6144].bitcast(BF16)]
            S2 = [bufA[:, 0:2048], bufA[:, 2048:4096]]
            A2b = [bufB[:, 0:1024].bitcast(BF16),
                   bufB[:, 1024:2048].bitcast(BF16),
                   bufB[:, 2048:3072].bitcast(BF16)]
            AT2 = [bufB[:, 3072:4096].bitcast(BF16),
                   bufB[:, 4096:5120].bitcast(BF16),
                   bufB[:, 5120:6144].bitcast(BF16)]
            avst = bufE[:, 64:320].bitcast(BF16)
            nmax2 = [bufE[:, 8:9], bufE[:, 9:10], bufE[:, 14:15]]
            lsum2 = [bufE[:, 10:11], bufE[:, 11:12], bufE[:, 15:16]]
            rinv2 = [bufE[:, 12:13], bufE[:, 13:14], bufE[:, 16:17]]
            rrow2 = [bufE[0:1, 1216:1344], bufE[0:1, 1472:1600 - 64]
                     if False else bufE[0:1, 1344:1472]]
            rrow2 = [bufE[0:1, 1216:1344], bufE[0:1, 1344:1472]]
            rbc2 = [bufE[:, 704:832], bufE[:, 832:960]]
            bs = dict(cons=[], avev=[], avst=[], uc=0, bankfree={},
                      exv2={}, atpe2={}, avpe2={}, rrowd2={}, rbcrd2={})
            pend = [None]
            pend2 = [None]
            pend3 = [None]
            stq_tot = [(s_stq[0], s_stq[0].n), (s_stq[1], s_stq[1].n),
                       (s_stv[0], s_stv[0].n), (s_stv[1], s_stv[1].n)]

            def emit_atav(pu):
                iq = pu["iq"]
                par = pu["par"]
                p3 = pu["p3"]
                si = pu["si"]
                hh = pu["hh"]
                for kc in range(iq + 1):
                    slot = kc % 8
                    wts2 = []
                    if kc == 0:
                        wts2.append((s_h, pu["sa_done"]))
                    if slot == 0 and bs.get("atev") is not None:
                        wts2.append((s_tpe, bs["atev"]))
                    em("tensor", lambda e, kc=kc, slot=slot, p3=p3:
                       e.transpose(
                           psG[:, slot * P:(slot + 1) * P],
                           A2b[p3][:, kc * P:(kc + 1) * P], cs_identb[:]),
                       waits=wts2,
                       inc=s_tp if (slot == 7 or kc == iq) else None)
                    if slot == 7 or kc == iq:
                        lo = kc - slot
                        em("scalar", lambda e, lo=lo, slot=slot, p3=p3:
                           e.copy(
                               AT2[p3][:, lo * P:(lo + slot + 1) * P],
                               psG[:, 0:(slot + 1) * P]),
                           waits=[(s_tp, s_tp.n)], inc=s_tpe)
                        bs["atev"] = s_tpe.n
                bs["atpe2"][p3] = s_tp.n
                at_done = s_tpe.n
                bank = (psE, psF)[pu["uc"] % 2]
                wts3 = [(s_tpe, at_done)]
                if len(bs["avev"]) >= 2:
                    wts3.append((s_evA, bs["avev"][-2]))
                for kc in range(iq + 1):
                    em("tensor", lambda e, kc=kc, par=par, bank=bank,
                       iq=iq, p3=p3: e.matmul(
                           bank[:, 0:P],
                           vth[par][:, kc * P:(kc + 1) * P],
                           AT2[p3][:, kc * P:(kc + 1) * P],
                           start=(kc == 0), stop=(kc == iq)),
                       waits=wts3 if kc == 0 else [],
                       inc=s_tp if kc == iq else None)
                av_done = s_tp.n
                bs["avpe2"][p3] = av_done
                sslot = iq % 4
                wts4 = [(s_tp, av_done)]
                if sslot == 0 and len(bs["avst"]) >= 1:
                    wts4.append((s_sta, bs["avst"][-1]))
                em("scalar", lambda e, sslot=sslot, bank=bank:
                   e.copy(avst[:, sslot * P:(sslot + 1) * P], bank[:, 0:P]),
                   waits=wts4, inc=s_evA)
                bs["avev"].append(s_evA.n)
                if sslot == 3:
                    lo = iq - 3
                    em("sync", lambda e, lo=lo, hh=hh, si=si:
                       e.dma_start(
                           out=avt_scr[hh * DH:(hh + 1) * DH,
                                       si * S + lo * P:
                                       si * S + (lo + 4) * P],
                           in_=avst[:]),
                       waits=[(s_evA, s_evA.n)], inc=s_sta, amt=16)
                    bs["avst"].append(s_sta.n)
                if iq == S // P - 1:
                    bs["cons"].append(s_evA.n)

            for si in range(1 if lite else NSEQ):
                for hh in range(3 if lite else NH):
                    hi = si * NH + hh
                    par = hi % 2
                    wts = list(stq_tot) if hi == 0 else []
                    if hi >= 2 and len(bs["cons"]) >= 1:
                        wts.append((s_evA, bs["cons"][-1]))
                    em("scalar", lambda e, par=par, hh=hh, si=si: e.dma_start(
                        out=qth[par][:],
                        in_=qt_scr[hh * DH:(hh + 1) * DH, si * S:(si + 1) * S]),
                       waits=wts, inc=s_qkp[par], amt=16)
                    em("scalar", lambda e, par=par, hh=hh, si=si: e.dma_start(
                        out=kth[par][:],
                        in_=kt_scr[hh * DH:(hh + 1) * DH, si * S:(si + 1) * S]),
                       inc=s_qkp[par], amt=16)
                    em("scalar", lambda e, par=par, hh=hh, si=si: e.dma_start(
                        out=vth[par][:].rearrange("p (c n) -> p c n", c=DC),
                        in_=v_scr[si * S:(si + 1) * S,
                                  hh * DH:(hh + 1) * DH].rearrange(
                                      "(c p) n -> p c n", p=P)),
                       inc=s_qkp[par], amt=16)
                    ld_hi = s_qkp[par].n

                    for iq in range(S // P):
                        uc = bs["uc"]
                        bs["uc"] += 1
                        p2 = uc % 2
                        p3 = uc % 3
                        W = (iq + 1) * P
                        nkb = (W + 511) // 512
                        first_extra = [(s_qkp[par], ld_hi)] if iq == 0 else []
                        mm_done = []
                        for kb in range(nkb):
                            wdt = min(512, W - kb * 512)
                            wts = list(first_extra) if kb == 0 else []
                            if kb in bs["bankfree"]:
                                wts.append(bs["bankfree"][kb])
                            em("tensor", lambda e, kb=kb, iq=iq, par=par,
                               wdt=wdt: e.matmul(
                                   ps_big[:, kb * 512:kb * 512 + wdt],
                                   qth[par][:, iq * P:(iq + 1) * P],
                                   kth[par][:, kb * 512:kb * 512 + wdt],
                                   start=True, stop=True),
                               waits=wts, inc=s_sc)
                            mm_done.append(s_sc.n)
                        ev_guard = [(s_evA, bs["exv2"][p2])] if p2 in bs["exv2"] else []
                        cp_act = 0
                        mk_done = None
                        for kb in range(nkb):
                            wdt = min(512, W - kb * 512)
                            plain = wdt - P if kb == nkb - 1 else wdt
                            wev = [(s_sc, mm_done[kb])] + (ev_guard if kb == 0 else [])
                            if plain > 0:
                                if kb % 2 == 0:
                                    em("scalar", lambda e, kb=kb, p2=p2,
                                       plain=plain: e.copy(
                                           S2[p2][:, kb * 512:kb * 512 + plain],
                                           ps_big[:, kb * 512:kb * 512 + plain]),
                                       waits=wev, inc=s_evA)
                                    cp_act = s_evA.n
                                    bs["bankfree"][kb] = (s_evA, cp_act)
                                else:
                                    em("vector", lambda e, kb=kb, p2=p2,
                                       plain=plain: e.tensor_copy(
                                           S2[p2][:, kb * 512:kb * 512 + plain],
                                           ps_big[:, kb * 512:kb * 512 + plain]),
                                       waits=wev, inc=s_h)
                                    cp_dve = s_h.n
                                    bs["bankfree"][kb] = (s_h, cp_dve)
                            if kb == nkb - 1:
                                wmk = [(s_sc, mm_done[kb])]
                                if kb == 0:
                                    wmk += ev_guard
                                if plain > 0 and kb % 2 == 0:
                                    wmk.append((s_evA, cp_act))
                                em("vector", lambda e, iq=iq, p2=p2:
                                   e.tensor_tensor(
                                       S2[p2][:, iq * P:(iq + 1) * P],
                                       ps_big[:, iq * P:(iq + 1) * P],
                                       cs_mask[:], op=ALU.add),
                                   waits=wmk, inc=s_h)
                                mk_done = s_h.n
                                bs["bankfree"][kb] = (s_h, mk_done)
                        pend2b = None
                        if pend[0] is not None:
                            pv = pend[0]
                            vp2 = pv["p2"]
                            vp3 = pv["p3"]
                            wex = [(s_h, pv["nm_done"])]
                            if vp3 in bs["atpe2"]:
                                wex.append((s_tp, bs["atpe2"][vp3]))
                            em("scalar", lambda e, W=pv["W"], p2=vp2, p3=vp3:
                               e.activation(
                                   A2b[p3][:, 0:W], S2[p2][:, 0:W], AF.Exp,
                                   bias=nmax2[p3], scale=1.0,
                                   accum_out=lsum2[p3]),
                               waits=wex, inc=s_evA)
                            pv["ex_done"] = s_evA.n
                            bs["exv2"][vp2] = pv["ex_done"]
                            em("vector", lambda e, p3=vp3: e.reciprocal(
                                rinv2[p3], lsum2[p3]),
                               waits=[(s_evA, pv["ex_done"])])
                            em("vector", lambda e, W=pv["W"], p3=vp3:
                               e.tensor_scalar_mul(
                                   A2b[p3][:, 0:W], A2b[p3][:, 0:W],
                                   rinv2[p3]), inc=s_h)
                            pv["sa_done"] = s_h.n
                            pend2b = pv
                        if pend3[0] is not None:
                            emit_atav(pend3[0])
                        pend3[0] = pend2[0]
                        pend2[0] = pend2b
                        wmx = [(s_h, mk_done)]
                        if cp_act:
                            wmx.append((s_evA, cp_act))
                        em("vector", lambda e, W=W, p2=p2, p3=p3: e.reduce_max(
                            nmax2[p3], S2[p2][:, 0:W], axis=AX.X, negate=True),
                           waits=wmx, inc=s_h)
                        nm_done = s_h.n
                        pend[0] = dict(iq=iq, par=par, p2=p2, p3=p3, si=si,
                                       hh=hh, uc=uc, nm_done=nm_done, W=W)

            if pend3[0] is not None:
                emit_atav(pend3[0])
                pend3[0] = None
            if pend2[0] is not None:
                emit_atav(pend2[0])
                pend2[0] = None
            if pend[0] is not None:
                pv = pend[0]
                vp2 = pv["p2"]
                vp3 = pv["p3"]
                wex = [(s_h, pv["nm_done"])]
                if vp3 in bs["atpe2"]:
                    wex.append((s_tp, bs["atpe2"][vp3]))
                em("scalar", lambda e, W=pv["W"], p2=vp2, p3=vp3: e.activation(
                    A2b[p3][:, 0:W], S2[p2][:, 0:W], AF.Exp,
                    bias=nmax2[p3], scale=1.0, accum_out=lsum2[p3]),
                   waits=wex, inc=s_evA)
                pv["ex_done"] = s_evA.n
                em("vector", lambda e, p3=vp3: e.reciprocal(
                    rinv2[p3], lsum2[p3]), waits=[(s_evA, pv["ex_done"])])
                em("vector", lambda e, W=pv["W"], p3=vp3: e.tensor_scalar_mul(
                    A2b[p3][:, 0:W], A2b[p3][:, 0:W], rinv2[p3]), inc=s_h)
                pv["sa_done"] = s_h.n
                emit_atav(pv)

        def oproj_pass():
            bob = bufB[:, 0:2048]
            tmp = bufB[:, 2048:2560]
            em("sync", lambda e: e.dma_start(out=bob, in_=bob_h[:]),
               waits=[(s_stx, s_stx.n)], inc=s_ld, amt=16)
            ldv = s_ld.n
            wow = [bufW1[:, 0:4096].bitcast(BF16),
                   bufW1[:, 4096:8192].bitcast(BF16)]
            avt = bufC[:].bitcast(BF16)
            c1 = dict(u=0, done=[], ev=[])
            c1["stw"] = []
            for t in range(2 if lite else NTILE):
                c1["av_base"] = s_av.n
                if c1.get("pref_for") != t:
                    wl1 = [(s_ld, ldv)]
                    if t == 0:
                        wl1 += [(s_stx, s_stx.n), (s_sta, s_sta.n)]
                    if c1.get("bufA_free"):
                        wl1.append((s_av, c1["bufA_free"]))
                    em("scalar", lambda e, t=t: e.dma_start(
                        out=bufA[:].rearrange("p (g f) -> p g f", g=NG),
                        in_=x1_scr[t * TT:(t + 1) * TT, :].rearrange(
                            "(g p) f -> p g f", p=P)),
                       waits=wl1, inc=s_x, amt=16)
                    em("scalar", lambda e, t=t: e.dma_start(
                        out=avt[:].rearrange("p (c n) -> p c n", c=DC),
                        in_=avt_scr.rearrange("(c p) n -> p c n", p=P)[
                            :, :, t * TT:(t + 1) * TT]),
                       inc=s_x, amt=16)
                    c1["x_next"] = s_x.n
                x_ready = c1["x_next"]
                for dg in range(4):
                    u = c1["u"]
                    c1["u"] += 1
                    wpar = u % 2
                    woff = 3 * 32768 + dg * 8192
                    wts = []
                    if len(c1["done"]) >= 2:
                        wts.append((s_tp, c1["done"][-2]))
                    if t == 0:
                        if len(c1["stw"]) >= 2:
                            wts.append(c1["stw"][-2])
                        em("gpsimd", lambda e, wpar=wpar, dg=dg: e.dma_start(
                            out=wow[wpar][:].rearrange("p (c n) -> p c n", c=DC),
                            in_=wo_h.rearrange("(c p) n -> p c n", p=P)[
                                :, :, dg * 512:(dg + 1) * 512]),
                           waits=wts, inc=s_vw[wpar], amt=16)
                        ld_rec = (s_vw[wpar], s_vw[wpar].n)
                        em("sync", lambda e, wpar=wpar, woff=woff: e.dma_start(
                            out=watt_scr[:, woff:woff + 8192], in_=wow[wpar][:]),
                           waits=[ld_rec], inc=s_vst[wpar], amt=16)
                        c1["stw"].append((s_vst[wpar], s_vst[wpar].n))
                    else:
                        if t == 1 and dg == 0:
                            wts += [(s_vst[0], s_vst[0].n),
                                    (s_vst[1], s_vst[1].n)]
                        em("sync", lambda e, wpar=wpar, woff=woff: e.dma_start(
                            out=wow[wpar][:], in_=watt_scr[:, woff:woff + 8192]),
                           waits=wts, inc=s_vl[wpar], amt=16)
                        ld_rec = (s_vl[wpar], s_vl[wpar].n)
                    unit_last = None
                    for tg in range(NG):
                        r2 = u * NG + tg
                        bank = (psA, psB)[r2 % 2]
                        wts2 = [ld_rec, (s_x, x_ready)] if tg == 0 else []
                        if len(c1["ev"]) >= 2:
                            wts2.append((s_av, c1["ev"][-2]))
                        for c in range(DC):
                            em("tensor", lambda e, c=c, tg=tg, wpar=wpar,
                               bank=bank, avt=avt: e.matmul(
                                   bank,
                                   avt.rearrange("p (c n) -> p c n", c=DC)[
                                       :, c, tg * P:(tg + 1) * P],
                                   wow[wpar][:, c * 512:(c + 1) * 512],
                                   start=(c == 0), stop=(c == DC - 1)),
                               waits=wts2 if c == 0 else [],
                               inc=s_tp if c == DC - 1 else None)
                        unit_last = s_tp.n
                        em("vector", lambda e, dg=dg, bank=bank:
                           e.tensor_tensor(
                               tmp, bank, bob[:, dg * 512:(dg + 1) * 512],
                               op=ALU.add), waits=[(s_tp, unit_last)])
                        sl = slice(tg * D + dg * 512, tg * D + (dg + 1) * 512)
                        wadd = []
                        if dg == 0 and tg == 0 and c1.get("st_prev"):
                            wadd.append((s_stx, c1["st_prev"]))
                        em("vector", lambda e, sl=sl: e.tensor_tensor(
                            bufW2[:, sl], bufA[:, sl], tmp, op=ALU.add),
                           waits=wadd, inc=s_av)
                        c1["ev"].append(s_av.n)
                    c1["done"].append(unit_last)
                    if dg == 1 and t + 1 < (2 if lite else NTILE):
                        avb = c1["av_base"] + 16
                        em("scalar", lambda e, t=t: e.dma_start(
                            out=bufA[:].rearrange("p (g f) -> p g f", g=NG),
                            in_=x1_scr[(t + 1) * TT:(t + 2) * TT, :].rearrange(
                                "(g p) f -> p g f", p=P)),
                           waits=[(s_av, avb)], inc=s_x, amt=16)
                        em("scalar", lambda e, t=t: e.dma_start(
                            out=avt[:].rearrange("p (c n) -> p c n", c=DC),
                            in_=avt_scr.rearrange("(c p) n -> p c n", p=P)[
                                :, :, (t + 1) * TT:(t + 2) * TT]),
                           inc=s_x, amt=16)
                        c1["x_next"] = s_x.n
                        c1["pref_for"] = t + 1
                em("sync", lambda e, t=t: e.dma_start(
                    out=x2_scr[t * TT:(t + 1) * TT, :].rearrange(
                        "(g p) f -> p g f", p=P),
                    in_=bufW2[:, 0:8192].rearrange("p (g f) -> p g f", g=NG)),
                   waits=[(s_av, s_av.n)], inc=s_stx, amt=16)
                c1["st_prev"] = s_stx.n
                c1["bufA_free"] = s_av.n

        ffn_pass(x_h, x1_scr, precast=True)
        barrier()
        qkv_pass()
        barrier()
        attn_pass()
        barrier()
        oproj_pass()
        barrier()
        ffn_pass(x2_scr, y_h, precast=False)

        em("gpsimd", lambda e: e.nop(), waits=[(s_stx, s_stx.n)])

        blk = st.enter_context(nc.Block())

        def mk_body(lst):
            def body(eng):
                for (waits, fn, inch, amt) in lst:
                    for (sm, v) in waits:
                        if v > 0:
                            eng.wait_ge(sm.h, v)
                    ins = fn(eng)
                    if inch is not None:
                        ins.then_inc(inch, amt)
            return body
        for ename in ENGINES:
            getattr(blk, ename)(mk_body(items[ename]))

    return nc


def _consts():
    ident = np.eye(P, dtype=np.float32)
    identb = np.eye(P, dtype=np.float32).astype(ml_dtypes.bfloat16)
    sel8 = np.zeros((8, 8 * P), dtype=np.float32)
    for e in range(8):
        sel8[e, e * P:(e + 1) * P] = 1.0
    mask = np.where(np.arange(P)[:, None] >= np.arange(P)[None, :],
                    np.float32(0.0), np.float32(-1e30)).astype(np.float32)
    ones1 = np.ones((1, P), dtype=np.float32)
    onesb = np.ones((P, TT), dtype=np.float32).astype(ml_dtypes.bfloat16)
    return ident, identb, sel8, mask, ones1, onesb


def make_in_maps(inp):
    ident, identb, sel8, mask, ones1, onesb = _consts()
    asc = np.ascontiguousarray
    gate = np.asarray(inp["L0_ffn_gate"])
    ebias = np.asarray(inp["L0_ffn_expert_bias"])
    common = dict(
        w1t=asc(np.asarray(inp["L0_ffn_w1"]).transpose(0, 2, 1)),
        w3t=asc(np.asarray(inp["L0_ffn_w3"]).transpose(0, 2, 1)),
        w2t=asc(np.asarray(inp["L0_ffn_w2"]).transpose(0, 2, 1)),
        sw1t=asc(np.asarray(inp["L0_ffn_shared_w1"]).T),
        sw3t=asc(np.asarray(inp["L0_ffn_shared_w3"]).T),
        sw2t=asc(np.asarray(inp["L0_ffn_shared_w2"]).T),
        wq=np.asarray(inp["L0_attn_Wq"]), wk=np.asarray(inp["L0_attn_Wk"]),
        wv=np.asarray(inp["L0_attn_Wv"]), wo=np.asarray(inp["L0_attn_Wo"]),
        identf=ident, identb=identb, sel8=sel8, mask=mask,
        gatei=asc(gate.T.reshape(DC, P, 8).transpose(1, 0, 2).reshape(P, DC * 8)),
        bias32=asc(np.broadcast_to(np.tile(ebias, 4)[None, :], (P, 32))),
        bq16=asc(np.asarray(inp["L0_attn_bq"]).reshape(DC, P).T * ISQ),
        bk16=asc(np.asarray(inp["L0_attn_bk"]).reshape(DC, P).T),
        lngb=asc(np.broadcast_to(np.asarray(inp["L0_attn_ln_g"])[None, :], (P, D))),
        lnbb=asc(np.broadcast_to(np.asarray(inp["L0_attn_ln_b"])[None, :], (P, D))),
        bvb=asc(np.broadcast_to(np.asarray(inp["L0_attn_bv"])[None, :], (P, D))),
        bob=asc(np.broadcast_to(np.asarray(inp["L0_attn_bo"])[None, :], (P, D))),
        ones1=ones1, onesb=onesb,
    )
    X = np.asarray(inp["L0_attn_X"]).reshape(16 * S, D)
    in_maps = []
    for c in range(8):
        m = dict(common)
        m["x"] = asc(X[c * NT:(c + 1) * NT])
        in_maps.append(m)
    return in_maps


def kernel(**inp):
    in_maps = make_in_maps(inp)
    nc = bass.Bass("TRN2", target_bir_lowering=False, debug=False)
    build(nc)
    res = run_bass_kernel_spmd(nc, in_maps, core_ids=list(range(8)))
    out = np.concatenate([np.asarray(r["y"]) for r in res.results], axis=0)
    return out.reshape(16, S, D)


# version 15 (final)
# speedup vs baseline: 1.0215x; 1.0215x over previous
import contextlib
import numpy as np
import ml_dtypes
import concourse.bass as bass
import concourse.mybir as mybir
from concourse.bass_utils import run_bass_kernel_spmd

F32 = mybir.dt.float32
BF16 = mybir.dt.bfloat16
AF = mybir.ActivationFunctionType
ALU = mybir.AluOpType
AX = mybir.AxisListType

P = 128
D = 2048
DC = D // P
NT = 4096
TT = 512
NTILE = NT // TT
NG = TT // P
NH = 16
DH = 128
S = 2048
NSEQ = 2
NE = 9
H = 1024
HC = H // P
QH = 256
INV_D = 1.0 / D
EPS = 1e-5
ISQ = 0.08838834764831845

ENGINES = ("sync", "gpsimd", "vector", "scalar", "tensor")

U13 = 8192
U2 = 4096
EXP_STRIDE = 4 * U13 + 4 * U2
WFF_TOT = NE * EXP_STRIDE


class Sem:
    __slots__ = ("h", "n")

    def __init__(self, h):
        self.h = h
        self.n = 0


def build(nc, dbg=False, lite=False):
    x_h = nc.dram_tensor("x", [NT, D], F32, kind="ExternalInput").ap()
    w1t_h = nc.dram_tensor("w1t", [8, D, H], F32, kind="ExternalInput").ap()
    w3t_h = nc.dram_tensor("w3t", [8, D, H], F32, kind="ExternalInput").ap()
    w2t_h = nc.dram_tensor("w2t", [8, H, D], F32, kind="ExternalInput").ap()
    sw1t_h = nc.dram_tensor("sw1t", [D, H], F32, kind="ExternalInput").ap()
    sw3t_h = nc.dram_tensor("sw3t", [D, H], F32, kind="ExternalInput").ap()
    sw2t_h = nc.dram_tensor("sw2t", [H, D], F32, kind="ExternalInput").ap()
    wq_h = nc.dram_tensor("wq", [D, D], F32, kind="ExternalInput").ap()
    wk_h = nc.dram_tensor("wk", [D, D], F32, kind="ExternalInput").ap()
    wv_h = nc.dram_tensor("wv", [D, D], F32, kind="ExternalInput").ap()
    wo_h = nc.dram_tensor("wo", [D, D], F32, kind="ExternalInput").ap()
    identf_h = nc.dram_tensor("identf", [P, P], F32, kind="ExternalInput").ap()
    identb_h = nc.dram_tensor("identb", [P, P], BF16, kind="ExternalInput").ap()
    sel8_h = nc.dram_tensor("sel8", [8, 8 * P], F32, kind="ExternalInput").ap()
    mask_h = nc.dram_tensor("mask", [P, P], F32, kind="ExternalInput").ap()
    gatei_h = nc.dram_tensor("gatei", [P, DC * 8], F32, kind="ExternalInput").ap()
    bias32_h = nc.dram_tensor("bias32", [P, 32], F32, kind="ExternalInput").ap()
    bq16_h = nc.dram_tensor("bq16", [P, DC], F32, kind="ExternalInput").ap()
    bk16_h = nc.dram_tensor("bk16", [P, DC], F32, kind="ExternalInput").ap()
    lng_h = nc.dram_tensor("lngb", [P, D], F32, kind="ExternalInput").ap()
    lnb_h = nc.dram_tensor("lnbb", [P, D], F32, kind="ExternalInput").ap()
    bvb_h = nc.dram_tensor("bvb", [P, D], F32, kind="ExternalInput").ap()
    bob_h = nc.dram_tensor("bob", [P, D], F32, kind="ExternalInput").ap()
    ones1_h = nc.dram_tensor("ones1", [1, P], F32, kind="ExternalInput").ap()
    onesb_h = nc.dram_tensor("onesb", [P, TT], BF16, kind="ExternalInput").ap()

    y_h = nc.dram_tensor("y", [NT, D], F32, kind="ExternalOutput").ap()

    skind = "ExternalOutput" if dbg else "Internal"
    x1_scr = nc.dram_tensor("x1_scr", [NT, D], F32, kind=skind).ap()
    x2_scr = nc.dram_tensor("x2_scr", [NT, D], F32, kind=skind).ap()
    qt_scr = nc.dram_tensor("qt_scr", [D, NT], BF16, kind=skind).ap()
    kt_scr = nc.dram_tensor("kt_scr", [D, NT], BF16, kind=skind).ap()
    v_scr = nc.dram_tensor("v_scr", [NT, D], BF16, kind=skind).ap()
    avt_scr = nc.dram_tensor("avt_scr", [D, NT], BF16, kind=skind).ap()
    wff_scr = nc.dram_tensor("wff_scr", [P, WFF_TOT], BF16, kind="Internal").ap()
    watt_scr = nc.dram_tensor("watt_scr", [P, 4 * 32768], BF16, kind="Internal").ap()

    items = {e: [] for e in ENGINES}
    chain = {}
    last_link = {}

    def em(eng, fn, waits=(), inc=None, amt=1, nolink=False):
        waits = list(waits)
        if eng in ("vector", "scalar") and amt != 16 and not nolink:
            prev = last_link.get(eng)
            if prev is not None:
                waits.append(prev)
            if inc is None:
                inc = chain[eng]
        if inc is not None:
            inc.n += amt
            if eng in ("vector", "scalar") and amt != 16 and not nolink:
                last_link[eng] = (inc, inc.n)
        items[eng].append((tuple(waits), fn, (inc.h if inc else None), amt))

    with contextlib.ExitStack() as st:
        def sb(name, shape, dt):
            return st.enter_context(nc.sbuf_tensor(name, shape, dt))

        bufA = sb("bufA", [P, 8192], F32)
        bufB = sb("bufB", [P, 8192], F32)
        bufC = sb("bufC", [P, 4096], F32)
        bufD = sb("bufD", [P, 2048], F32)
        bufE = sb("bufE", [P, 1536], F32)
        bufF = sb("bufF", [P, 2048], F32)
        bufW1 = sb("bufW1", [P, 8192], F32)
        bufW2 = sb("bufW2", [P, 8192], F32)
        bufW3 = sb("bufW3", [P, 7168], F32)

        cs_identf = sb("cs_identf", [P, P], F32)
        cs_identb = sb("cs_identb", [P, P], BF16)
        cs_sel8 = sb("cs_sel8", [8, 8 * P], F32)
        cs_mask = sb("cs_mask", [P, P], F32)
        cs_gatei = sb("cs_gatei", [P, DC * 8], F32)
        cs_bias32 = sb("cs_bias32", [P, 32], F32)
        cs_bq = sb("cs_bq", [P, DC], F32)
        cs_bk = sb("cs_bk", [P, DC], F32)
        cs_ones1 = sb("cs_ones1", [1, P], F32)
        cs_onesb = sb("cs_onesb", [P, TT], BF16)
        cs_eps = sb("cs_eps", [P, 1], F32)

        ps_big = st.enter_context(nc.psum_tensor("ps_big", [P, 2048], F32))[:]
        psE = st.enter_context(nc.psum_tensor("psE", [P, 512], F32))[:]
        psF = st.enter_context(nc.psum_tensor("psF", [P, 512], F32))[:]
        psG = st.enter_context(nc.psum_tensor("psG", [P, 1024], BF16))[:]
        psH = st.enter_context(nc.psum_tensor("psH", [P, 512], F32))[:]
        psGf = psG.bitcast(F32)
        psA = ps_big[:, 0:512]
        psB = ps_big[:, 512:1024]
        psC = ps_big[:, 1024:1536]
        psD = ps_big[:, 1536:2048]

        def sem(name):
            return Sem(st.enter_context(nc.semaphore(name)))

        chain["vector"] = sem("s_chv")
        chain["scalar"] = sem("s_chs")
        s_ld = sem("s_ld")
        s_x = sem("s_x")
        s_stx = sem("s_stx")
        s_sta = sem("s_sta")
        s_wlp = [sem("s_wlA"), sem("s_wlB"), sem("s_wlC")]
        s_w2lp = [sem("s_w2lA"), sem("s_w2lB")]
        s_wlc = [sem("s_wlcA"), sem("s_wlcB"), sem("s_wlcC")]
        s_w2c = [sem("s_w2cA"), sem("s_w2cB")]
        s_wst13 = [sem("s_wst13A"), sem("s_wst13B"), sem("s_wst13C")]
        s_wst2 = [sem("s_wst2A"), sem("s_wst2B")]
        s_stq = [sem("s_stqA"), sem("s_stqB")]
        s_stv = [sem("s_stvA"), sem("s_stvB")]
        s_qkp = [sem("s_qkA"), sem("s_qkB")]
        s_qkw = [sem("s_qkwA"), sem("s_qkwB"), sem("s_qkwC"), sem("s_qkwD")]
        s_vw = [sem("s_vwA"), sem("s_vwB")]
        s_qkl = [sem("s_qklA"), sem("s_qklB"), sem("s_qklC"), sem("s_qklD")]
        s_vl = [sem("s_vlA"), sem("s_vlB")]
        s_ast = [sem("s_astA"), sem("s_astB"), sem("s_astC"), sem("s_astD")]
        s_vst = [sem("s_vstA"), sem("s_vstB")]
        s_ab = sem("s_ab")
        s_evA = sem("s_evA")
        s_h = sem("s_h")
        s_w2p = sem("s_w2p")
        s_accd = sem("s_accd")
        s_ep = sem("s_ep")
        s_rt = sem("s_rt")
        s_tp = sem("s_tp")
        s_tpe = sem("s_tpe")
        s_gb = sem("s_gb")
        s_xte = sem("s_xte")
        s_pb = sem("s_pb")
        s_sc = sem("s_sc")
        s_av = sem("s_av")

        def barrier():
            for e in ENGINES:
                em(e, lambda en: en.nop(), inc=s_pb, nolink=True)
            tgt = s_pb.n
            for e in ENGINES:
                em(e, lambda en: en.nop(), waits=[(s_pb, tgt)], nolink=True)

        for dst, src in ((cs_identf, identf_h), (cs_identb, identb_h),
                         (cs_sel8, sel8_h), (cs_mask, mask_h),
                         (cs_gatei, gatei_h), (cs_bias32, bias32_h),
                         (cs_bq, bq16_h), (cs_bk, bk16_h),
                         (cs_ones1, ones1_h), (cs_onesb, onesb_h)):
            em("sync", lambda e, d=dst, s=src: e.dma_start(out=d[:], in_=s[:]),
               inc=s_ld, amt=16)
        em("vector", lambda e: e.memset(cs_eps[:], EPS), inc=s_ld)
        ld0 = s_ld.n
        for e in ENGINES:
            em(e, lambda en: en.nop(), waits=[(s_ld, ld0)])

        xnT = bufC[:].bitcast(BF16)
        xsT = [bufW1[:, 0:4096].bitcast(BF16)]
        w13 = [bufW2[:, 0:4096].bitcast(BF16),
               bufW2[:, 4096:8192].bitcast(BF16),
               bufW1[:, 4096:8192].bitcast(BF16)]
        w2q = [bufW3[:, 0:2048].bitcast(BF16),
               bufW3[:, 2048:4096].bitcast(BF16)]
        ha2 = [bufW3[:, 4096:4352].bitcast(BF16),
               bufW3[:, 4352:4608].bitcast(BF16)]
        h_sb = bufW3[:, 5120:7168].bitcast(BF16)
        gb_all = bufF[:].bitcast(BF16)
        sq_ffn = bufW1[:, 0:2048]
        sq_a2 = bufB[:, 6144:8192]

        mean_c = bufE[:, 0:1]
        var_c = bufE[:, 1:2]
        std_c = bufE[:, 2:3]
        rstd_c = bufE[:, 3:4]
        mean2_c = bufE[:, 4:5]
        var2_c = bufE[:, 5:6]
        std2_c = bufE[:, 6:7]
        rstd2_c = bufE[:, 7:8]
        nmax_c = bufE[:, 8:9]
        lsum_c = bufE[:, 9:10]
        rinv_c = bufE[:, 10:11]
        bnst = bufE[:, 24:48]
        rt_tmp = bufE[:, 64:576]
        t8_sb = bufE[:, 640:648]
        m1_c = bufE[:, 648:649]
        m2_c = bufE[:, 649:650]
        gT_sb = bufE[0:8, 704:1216]
        rrow_sb = bufE[0:1, 1216:1344]
        rbc_sb = bufE[:, 1344:1472]

        def s_col(g):
            return bufE[:, 576 + g * 8:576 + (g + 1) * 8]

        def sp_col(g):
            return bufE[:, 608 + g * 8:608 + (g + 1) * 8]

        def g_col(g):
            return bufE[:, 656 + g * 8:656 + (g + 1) * 8]

        gs = dict(k=0, j=0, gi=0, r=0, k_ld=[], j_ld=[], e_ab=[],
                  evA=[], hml=[], tp_evac=None, wst13=[], wst2=[],
                  wst_done=False)

        def ln_plain(xg, out_ap, sq_ap, sq_guard):
            em("vector", lambda e: e.reduce_sum(mean_c, xg, axis=AX.X))
            em("vector", lambda e: e.tensor_scalar_mul(mean_c, mean_c, INV_D))
            em("vector", lambda e: e.tensor_scalar(
                out=out_ap, in0=xg, scalar1=mean_c, scalar2=None,
                op0=ALU.subtract), inc=s_rt)
            em("scalar", lambda e: e.square(sq_ap, out_ap),
               waits=[(s_rt, s_rt.n)] + sq_guard, inc=s_rt)
            em("vector", lambda e: e.reduce_sum(var_c, sq_ap, axis=AX.X),
               waits=[(s_rt, s_rt.n)])
            em("vector", lambda e: e.tensor_scalar_mul(var_c, var_c, INV_D),
               inc=s_rt)
            em("scalar", lambda e: e.activation(
                std_c, var_c, AF.Sqrt, bias=cs_eps[:], scale=1.0),
               waits=[(s_rt, s_rt.n)], inc=s_rt)
            em("vector", lambda e: e.reciprocal(rstd_c, std_c),
               waits=[(s_rt, s_rt.n)])
            em("vector", lambda e: e.tensor_scalar(
                out=out_ap, in0=xg, scalar1=mean_c, scalar2=rstd_c,
                op0=ALU.subtract, op1=ALU.mult), inc=s_rt)

        def transpose_group(g, src2048, dst_bf, extra_f32=None, router_g=None,
                            evac_wait=None, router_wait=None):
            xn_done = s_rt.n
            for b in range(4):
                wts = [(s_rt, xn_done)] if b == 0 else []
                if gs["tp_evac"] is not None:
                    wts = wts + [(s_tpe, gs["tp_evac"])]
                for i in range(4):
                    c = b * 4 + i
                    em("tensor", lambda e, c=c, i=i, src2048=src2048: e.transpose(
                        psGf[:, i * P:(i + 1) * P],
                        src2048[:, c * P:(c + 1) * P], cs_identf[:]),
                       waits=wts if i == 0 else [],
                       inc=s_tp if i == 3 else None)
                tpv = s_tp.n
                ewt = [(s_tp, tpv)]
                if b == 0 and g == 0 and evac_wait is not None:
                    ewt.append(evac_wait)
                em("scalar", lambda e, b=b, g=g, dst_bf=dst_bf: e.copy(
                    dst_bf.rearrange("p (c n) -> p c n", c=DC)[
                        :, b * 4:(b + 1) * 4, g * P:(g + 1) * P],
                    psGf[:].rearrange("p (c n) -> p c n", c=4)),
                   waits=ewt,
                   inc=None if router_g is not None else s_tpe)
                if router_g is not None:
                    em("scalar", lambda e: e.copy(rt_tmp, psGf[:]),
                       inc=s_tpe)
                    evv = s_tpe.n
                    for i in range(4):
                        cc = b * 4 + i
                        rwt = [(s_tpe, evv)] if i == 0 else []
                        if b == 0 and i == 0 and g == 0 and router_wait is not None:
                            rwt.append(router_wait)
                        em("tensor", lambda e, i=i, cc=cc, g=g, b=b: e.matmul(
                            psH[:, g * 8:(g + 1) * 8],
                            rt_tmp[:, i * P:(i + 1) * P],
                            cs_gatei[:, cc * 8:(cc + 1) * 8],
                            start=(b == 0 and i == 0),
                            stop=(b == 3 and i == 3)),
                           waits=rwt,
                           inc=s_rt if (b == 3 and i == 3) else None)
                gs["tp_evac"] = s_tpe.n

        def ffn_pass(src_ap, dst_ap, precast):
            for t in range(2 if lite else NTILE):
                if gs.get("prol_for") != (id(src_ap), t):
                    wts = []
                    if gs.get("sh_acc"):
                        wts.append((s_accd, gs["sh_acc"]))
                    if t == 0 and s_stx.n:
                        wts.append((s_stx, s_stx.n))
                    em("sync", lambda e, t=t, src=src_ap: e.dma_start(
                        out=bufA[:].rearrange("p (g f) -> p g f", g=NG),
                        in_=src[t * TT:(t + 1) * TT, :].rearrange(
                            "(g p) f -> p g f", p=P)),
                       waits=wts, inc=s_x, amt=16)
                x_ready = s_x.n

                first_nonpre = (not gs["wst_done"]) and not (precast and t == 0) \
                    and len(gs["wst13"]) > 0
                accd_base = s_accd.n
                ntile_eff2 = 2 if lite else NTILE
                for e9 in ([8] + list(range(8))):
                    for q in range(4):
                        k = gs["k"]
                        gs["k"] += 1
                        par = k % 3
                        off = e9 * EXP_STRIDE + q * U13
                        wts = []
                        if k >= 3:
                            wts.append((s_ab, 4 * (k - 2)))
                        if precast and t == 0:
                            if k >= 3:
                                wts.append((s_wst13[par], gs["wst13"][k - 3]))
                            s1 = w1t_h[e9] if e9 < 8 else sw1t_h
                            s3 = w3t_h[e9] if e9 < 8 else sw3t_h
                            em("gpsimd", lambda e, par=par, s1=s1, q=q: e.dma_start(
                                out=w13[par][:, 0:4096].rearrange(
                                    "p (c n) -> p c n", c=DC),
                                in_=s1.rearrange("(c p) n -> p c n", p=P)[
                                    :, :, q * QH:(q + 1) * QH]),
                               waits=wts, inc=s_wlc[par], amt=16)
                            em("gpsimd", lambda e, par=par, s3=s3, q=q: e.dma_start(
                                out=w13[par][:, 4096:8192].rearrange(
                                    "p (c n) -> p c n", c=DC),
                                in_=s3.rearrange("(c p) n -> p c n", p=P)[
                                    :, :, q * QH:(q + 1) * QH]),
                               inc=s_wlc[par], amt=16)
                            em("sync", lambda e, par=par, off=off: e.dma_start(
                                out=wff_scr[:, off:off + U13], in_=w13[par][:]),
                               waits=[(s_wlc[par], s_wlc[par].n)],
                               inc=s_wst13[par], amt=16)
                            gs["wst13"].append(s_wst13[par].n)
                            gs["k_ld"].append((s_wlc[par], s_wlc[par].n))
                        else:
                            if first_nonpre:
                                wts += [(s, s.n) for s in s_wst13]
                                wts += [(s, s.n) for s in s_wst2]
                                first_nonpre = False
                                gs["wst_done"] = True
                            em("sync", lambda e, par=par, off=off: e.dma_start(
                                out=w13[par][:], in_=wff_scr[:, off:off + U13]),
                               waits=wts, inc=s_wlp[par], amt=16)
                            gs["k_ld"].append((s_wlp[par], s_wlp[par].n))
                    for q in range(4):
                        j = gs["j"]
                        gs["j"] += 1
                        par = j % 2
                        off = e9 * EXP_STRIDE + 4 * U13 + q * U2
                        wts = []
                        if j >= 2:
                            wts.append((s_accd, 4 * (j - 1)))
                        if precast and t == 0:
                            if j >= 2:
                                wts.append((s_wst2[par], gs["wst2"][j - 2]))
                            s2 = w2t_h[e9] if e9 < 8 else sw2t_h
                            em("gpsimd", lambda e, par=par, s2=s2, q=q: e.dma_start(
                                out=w2q[par][:].rearrange(
                                    "p (c n) -> p c n", c=HC),
                                in_=s2.rearrange("(c p) n -> p c n", p=P)[
                                    :, :, q * 512:(q + 1) * 512]),
                               waits=wts, inc=s_w2c[par], amt=16)
                            em("sync", lambda e, par=par, off=off: e.dma_start(
                                out=wff_scr[:, off:off + U2], in_=w2q[par][:]),
                               waits=[(s_w2c[par], s_w2c[par].n)],
                               inc=s_wst2[par], amt=16)
                            gs["wst2"].append(s_wst2[par].n)
                            gs["j_ld"].append((s_w2c[par], s_w2c[par].n))
                        else:
                            em("sync", lambda e, par=par, off=off: e.dma_start(
                                out=w2q[par][:], in_=wff_scr[:, off:off + U2]),
                               waits=wts, inc=s_w2lp[par], amt=16)
                            gs["j_ld"].append((s_w2lp[par], s_w2lp[par].n))
                    if e9 == 0 and t + 1 < ntile_eff2:
                        em("sync", lambda e, t=t, src=src_ap: e.dma_start(
                            out=bufA[:].rearrange("p (g f) -> p g f", g=NG),
                            in_=src[(t + 1) * TT:(t + 2) * TT, :].rearrange(
                                "(g p) f -> p g f", p=P)),
                           waits=[(s_accd, accd_base + 16)], inc=s_x, amt=16)
                        gs["x_next"] = s_x.n

                def prologue(x_ready_v, hoisted):
                    ew = (s_xte, gs["xte_v"][8]) if (hoisted and 8 in gs["xte_v"]) else None
                    rw = (s_gb, s_gb.n) if hoisted else None
                    for g in range(NG):
                        xg = bufA[:, g * D:(g + 1) * D]
                        w0 = [(s_x, x_ready_v)] if g == 0 else []
                        if g > 0 or hoisted:
                            w0.append((s_tp, s_tp.n))
                        for c in range(4):
                            em("vector", lambda e, xg=xg, c=c: e.bn_stats(
                                bnst[:, c * 6:(c + 1) * 6],
                                xg[:, c * 512:(c + 1) * 512]),
                               waits=w0 if c == 0 else [])
                        em("vector", lambda e: e.bn_aggr(bufE[:, 0:2], bnst))
                        em("scalar", lambda e: e.activation(
                            std_c, var_c, AF.Sqrt, bias=cs_eps[:], scale=1.0),
                           waits=[(chain["vector"], chain["vector"].n)],
                           inc=s_rt)
                        em("vector", lambda e: e.reciprocal(rstd_c, std_c),
                           waits=[(s_rt, s_rt.n)])
                        em("vector", lambda e, xg=xg: e.tensor_scalar(
                            out=bufD[:], in0=xg, scalar1=mean_c,
                            scalar2=rstd_c,
                            op0=ALU.subtract, op1=ALU.mult), inc=s_rt)
                        transpose_group(g, bufD[:], xnT, router_g=g,
                                        evac_wait=ew if g == 0 else None,
                                        router_wait=rw if g == 0 else None)
                    gs["xnt_ready"] = gs["tp_evac"]

                if gs.get("prol_for") != (id(src_ap), t):
                    prologue(x_ready, False)
                xnt_ready = gs["xnt_ready"]

                def scale_block(idx, e9, scale_wts):
                    par = 0
                    wts = list(scale_wts)
                    if idx >= 1 and len(gs["e_ab"]) >= 1:
                        wts.append((s_ab, gs["e_ab"][-1]))
                    gsrc = cs_onesb[:] if e9 == 8 else \
                        gb_all[:, e9 * TT:(e9 + 1) * TT]
                    for c in range(DC):
                        em("vector", lambda e, c=c, par=par, gsrc=gsrc:
                           e.tensor_tensor(
                               xsT[par][:, c * TT:(c + 1) * TT],
                               xnT[:, c * TT:(c + 1) * TT], gsrc, op=ALU.mult),
                           waits=wts if c == 0 else [],
                           inc=s_xte if c == DC - 1 else None)
                    gs["xte_v"][idx] = s_xte.n

                def expert_block(idx, e9, nxt=None):
                    par = 0
                    xte_done = gs["xte_v"][idx]
                    for q in range(4):
                        kk = len(gs["k_ld"]) - (NE - idx) * 4 + q
                        ld_need = gs["k_ld"][kk]
                        wpar = kk % 3
                        for hh in range(2):
                            gi = gs["gi"]
                            gs["gi"] += 1
                            a_bank = (psA, psC)[gi % 2]
                            b_bank = (psB, psD)[gi % 2]
                            wts = []
                            if q == 0 and hh == 0:
                                wts.append((s_xte, xte_done))
                            if hh == 0:
                                wts.append(ld_need)
                            if len(gs["evA"]) >= 2:
                                wts.append((s_evA, gs["evA"][-2]))
                            for c in range(DC):
                                em("tensor", lambda e, c=c, hh=hh, wpar=wpar,
                                   par=par, a_bank=a_bank: e.matmul(
                                       a_bank,
                                       w13[wpar][:, c * QH + hh * P:
                                                 c * QH + (hh + 1) * P],
                                       xsT[par][:, c * TT:(c + 1) * TT],
                                       start=(c == 0), stop=(c == DC - 1)),
                                   waits=wts if c == 0 else [],
                                   inc=s_ab if c == DC - 1 else None)
                            wts2 = []
                            if len(gs["hml"]) >= 2:
                                wts2.append((s_h, gs["hml"][-2]))
                            for c in range(DC):
                                em("tensor", lambda e, c=c, hh=hh, wpar=wpar,
                                   par=par, b_bank=b_bank: e.matmul(
                                       b_bank,
                                       w13[wpar][:, 4096 + c * QH + hh * P:
                                                 4096 + c * QH + (hh + 1) * P],
                                       xsT[par][:, c * TT:(c + 1) * TT],
                                       start=(c == 0), stop=(c == DC - 1)),
                                   waits=wts2 if c == 0 else [],
                                   inc=s_ab if c == DC - 1 else None)
                            ab_now = s_ab.n
                            silu_w = [(s_ab, ab_now - 1)]
                            if len(gs["hml"]) >= 2:
                                silu_w.append((s_h, gs["hml"][-2]))
                            em("scalar", lambda e, gi=gi, a_bank=a_bank:
                               e.activation(ha2[gi % 2][:], a_bank, AF.Silu),
                               waits=silu_w, inc=s_evA)
                            gs["evA"].append(s_evA.n)
                            hc_i = q * 2 + hh
                            em("vector", lambda e, gi=gi, hc_i=hc_i,
                               b_bank=b_bank: e.tensor_tensor(
                                   h_sb[:, hc_i * TT:(hc_i + 1) * TT],
                                   b_bank, ha2[gi % 2][:], op=ALU.mult),
                               waits=[(s_ab, ab_now), (s_evA, s_evA.n)],
                               inc=s_h)
                            gs["hml"].append(s_h.n)
                    gs["e_ab"].append(s_ab.n)
                    h_all = s_h.n
                    if nxt is not None:
                        scale_block(nxt[0], nxt[1], [(s_gb, gb_done)])

                    for q2 in range(4):
                        jj = len(gs["j_ld"]) - (NE - idx) * 4 + q2
                        ld_need = gs["j_ld"][jj]
                        wpar = jj % 2
                        for tg in range(NG):
                            r = gs["r"]
                            gs["r"] += 1
                            bank = (psE, psF)[r % 2]
                            wts = []
                            if q2 == 0 and tg == 0:
                                wts.append((s_h, h_all))
                            if tg == 0:
                                wts.append(ld_need)
                            if r >= 2:
                                wts.append((s_accd, r - 1))
                            for c in range(HC):
                                em("tensor", lambda e, c=c, tg=tg, wpar=wpar,
                                   bank=bank: e.matmul(
                                       bank,
                                       h_sb.rearrange("p (c n) -> p c n", c=HC)[
                                           :, c, tg * P:(tg + 1) * P],
                                       w2q[wpar][:, c * 512:(c + 1) * 512],
                                       start=(c == 0), stop=(c == HC - 1)),
                                   waits=wts if c == 0 else [],
                                   inc=s_w2p if c == HC - 1 else None)
                            accsl = bufB[:, tg * D + q2 * 512:
                                         tg * D + (q2 + 1) * 512]
                            if idx == 0:
                                w0 = [(s_w2p, s_w2p.n)]
                                if q2 == 0 and tg == 0 and gs.get("st_prev"):
                                    w0.append((s_stx, gs["st_prev"]))
                                xsl = bufA[:, tg * D + q2 * 512:
                                           tg * D + (q2 + 1) * 512]
                                em("vector", lambda e, accsl=accsl, bank=bank,
                                   xsl=xsl: e.scalar_tensor_tensor(
                                       out=accsl, in0=xsl, scalar=2.0,
                                       in1=bank, op0=ALU.mult, op1=ALU.add),
                                   waits=w0, inc=s_accd)
                            else:
                                em("vector", lambda e, accsl=accsl, bank=bank:
                                   e.tensor_tensor(accsl, accsl, bank,
                                                   op=ALU.add),
                                   waits=[(s_w2p, s_w2p.n)], inc=s_accd)

                gs["xte_v"] = {}
                scale_block(0, 8, [(s_tpe, xnt_ready)])
                expert_block(0, 8)
                gs["sh_acc"] = s_accd.n

                em("scalar", lambda e: e.activation(
                    bufE[:, 576:608], psH[:, 0:32], AF.Sigmoid),
                   waits=[(s_rt, s_rt.n)], inc=s_rt)
                em("vector", lambda e: e.tensor_add(
                    bufE[:, 608:640], bufE[:, 576:608], cs_bias32[:]),
                   waits=[(s_rt, s_rt.n)])
                for g in range(NG):
                    em("vector", lambda e, g=g: e.reduce_max(
                        m1_c, sp_col(g), axis=AX.X))
                    em("vector", lambda e, g=g: e.tensor_single_scalar(
                        out=t8_sb, in_=sp_col(g), scalar=m1_c, op=ALU.is_ge))
                    em("vector", lambda e, g=g: e.scalar_tensor_tensor(
                        out=t8_sb, in0=t8_sb, scalar=-1e30, in1=sp_col(g),
                        op0=ALU.mult, op1=ALU.add))
                    em("vector", lambda e: e.reduce_max(m2_c, t8_sb, axis=AX.X))
                    em("vector", lambda e, g=g: e.tensor_single_scalar(
                        out=t8_sb, in_=sp_col(g), scalar=m2_c, op=ALU.is_ge))
                    em("vector", lambda e, g=g: e.tensor_mul(
                        g_col(g), s_col(g), t8_sb), inc=s_rt)
                gates_done = s_rt.n
                for g in range(NG):
                    em("tensor", lambda e, g=g: e.transpose(
                        psH[0:8, g * P:(g + 1) * P], g_col(g), cs_identf[:]),
                       waits=[(s_rt, gates_done)] if g == 0 else [],
                       inc=s_tp if g == 3 else None)
                em("scalar", lambda e: e.copy(gT_sb, psH[0:8, :]),
                   waits=[(s_tp, s_tp.n)], inc=s_gb)
                for e8 in range(8):
                    em("tensor", lambda e, e8=e8: e.matmul(
                        psH[:], cs_sel8[:, e8 * P:(e8 + 1) * P], gT_sb,
                        start=True, stop=True),
                       waits=[(s_gb, s_gb.n)], inc=s_tp)
                    em("scalar", lambda e, e8=e8: e.copy(
                        gb_all[:, e8 * TT:(e8 + 1) * TT], psH[:]),
                       waits=[(s_tp, s_tp.n)], inc=s_gb)
                gb_done = s_gb.n

                scale_block(1, 0, [(s_gb, gb_done)])
                ntile_eff = 2 if lite else NTILE
                for e8 in range(8):
                    expert_block(e8 + 1, e8,
                                 nxt=(e8 + 2, e8 + 1) if e8 < 7 else None)
                    if e8 == 6 and t + 1 < ntile_eff:
                        prologue(gs["x_next"], True)
                        gs["prol_for"] = (id(src_ap), t + 1)

                acc_all = s_accd.n
                for tg in range(NG):
                    for q2 in range(4):
                        sl = slice(tg * D + q2 * 512, tg * D + (q2 + 1) * 512)
                        em("vector", lambda e, sl=sl: e.tensor_scalar_mul(
                            bufB[:, sl], bufB[:, sl], 0.5),
                           waits=[(s_accd, acc_all)] if (tg == 0 and q2 == 0)
                           else [],
                           inc=s_ep if (tg == 3 and q2 == 3) else None)
                em("scalar", lambda e, t=t, dst=dst_ap: e.dma_start(
                    out=dst[t * TT:(t + 1) * TT, :].rearrange(
                        "(g p) f -> p g f", p=P),
                    in_=bufB[:].rearrange("p (g f) -> p g f", g=NG)),
                   waits=[(s_ep, s_ep.n)], inc=s_stx, amt=16)
                gs["st_prev"] = s_stx.n

        def qkv_pass():
            lngb = bufB[:, 0:2048]
            lnbb = bufB[:, 2048:4096]
            bvb = bufB[:, 4096:6144]
            for dst, src in ((lngb, lng_h), (lnbb, lnb_h), (bvb, bvb_h)):
                em("sync", lambda e, d=dst, s=src: e.dma_start(out=d, in_=s[:]),
                   waits=[(s_stx, s_stx.n)], inc=s_ld, amt=16)
            ldv = s_ld.n
            qkw = [bufW2[:, 0:1024].bitcast(BF16),
                   bufW2[:, 1024:2048].bitcast(BF16),
                   bufW2[:, 2048:3072].bitcast(BF16),
                   bufW2[:, 3072:4096].bitcast(BF16)]
            vw = [bufW1[:, 0:4096].bitcast(BF16),
                  bufW1[:, 4096:8192].bitcast(BF16)]
            qstage = [bufE[:, 64:320].bitcast(BF16),
                      bufE[:, 320:576].bitcast(BF16)]
            vstage = [bufE[:, 576:832].bitcast(BF16),
                      bufE[:, 832:1088].bitcast(BF16)]
            XnT = bufC[:].bitcast(BF16)
            a2 = dict(qk_u=0, v_u=0, qk_done=[], qk_ev=[], qk_st=[],
                      v_done=[], v_ev=[], v_st=[], qk_stw=[], v_stw=[],
                      qk_u0=32)

            for t in range(2 if lite else NTILE):
                wtsx = [(s_ld, ldv), (s_stx, s_stx.n)]
                if a2.get("prol"):
                    wtsx.append((s_rt, a2["prol"]))
                em("scalar", lambda e, t=t: e.dma_start(
                    out=bufA[:].rearrange("p (g f) -> p g f", g=NG),
                    in_=x1_scr[t * TT:(t + 1) * TT, :].rearrange(
                        "(g p) f -> p g f", p=P)),
                   waits=wtsx, inc=s_x, amt=16)
                x_ready = s_x.n
                for g in range(NG):
                    xg = bufA[:, g * D:(g + 1) * D]
                    w0 = [(s_x, x_ready)] if g == 0 else []
                    if g > 0:
                        w0.append((s_tp, s_tp.n))
                    for c in range(4):
                        em("vector", lambda e, xg=xg, c=c: e.bn_stats(
                            bnst[:, c * 6:(c + 1) * 6],
                            xg[:, c * 512:(c + 1) * 512]),
                           waits=w0 if c == 0 else [])
                    em("vector", lambda e: e.bn_aggr(bufE[:, 0:2], bnst))
                    em("scalar", lambda e: e.activation(
                        std_c, var_c, AF.Sqrt, bias=cs_eps[:], scale=1.0),
                       waits=[(chain["vector"], chain["vector"].n)], inc=s_rt)
                    em("vector", lambda e: e.reciprocal(rstd_c, std_c),
                       waits=[(s_rt, s_rt.n)])
                    em("vector", lambda e, xg=xg: e.tensor_scalar(
                        out=xg, in0=xg, scalar1=mean_c, scalar2=rstd_c,
                        op0=ALU.subtract, op1=ALU.mult))
                    for c in range(4):
                        em("vector", lambda e, xg=xg, c=c: e.bn_stats(
                            bnst[:, c * 6:(c + 1) * 6],
                            xg[:, c * 512:(c + 1) * 512]))
                    em("vector", lambda e: e.bn_aggr(bufE[:, 4:6], bnst))
                    em("scalar", lambda e: e.activation(
                        std2_c, var2_c, AF.Sqrt, bias=cs_eps[:], scale=1.0),
                       waits=[(chain["vector"], chain["vector"].n)], inc=s_rt)
                    em("vector", lambda e: e.reciprocal(rstd2_c, std2_c),
                       waits=[(s_rt, s_rt.n)])
                    em("vector", lambda e, xg=xg: e.tensor_scalar(
                        out=bufD[:], in0=xg, scalar1=mean2_c, scalar2=rstd2_c,
                        op0=ALU.subtract, op1=ALU.mult))
                    em("vector", lambda e: e.tensor_mul(bufD[:], bufD[:], lngb))
                    em("vector", lambda e: e.tensor_add(bufD[:], bufD[:], lnbb),
                       inc=s_rt)
                    transpose_group(g, bufD[:], XnT)
                a2["prol"] = s_rt.n
                xnt_done = s_tpe.n

                for wi, (w_src, scr, bias_t) in enumerate(
                        ((wq_h, qt_scr, cs_bq), (wk_h, kt_scr, cs_bk))):
                    for dc in range(DC):
                        u = a2["qk_u"]
                        a2["qk_u"] += 1
                        wpar = u % 4
                        woff = wi * 32768 + dc * 2048
                        wts = []
                        if len(a2["qk_done"]) >= 4:
                            wts.append((s_tp, a2["qk_done"][-4]))
                        if t == 0:
                            if len(a2["qk_stw"]) >= 4:
                                wts.append(a2["qk_stw"][-4])
                            em("gpsimd", lambda e, wpar=wpar, dc=dc, ws=w_src:
                               e.dma_start(
                                   out=qkw[wpar][:].rearrange(
                                       "p (c n) -> p c n", c=DC),
                                   in_=ws.rearrange("(c p) n -> p c n", p=P)[
                                       :, :, dc * P:(dc + 1) * P]),
                               waits=wts, inc=s_qkw[wpar], amt=16)
                            ld_rec = (s_qkw[wpar], s_qkw[wpar].n)
                            em("sync", lambda e, wpar=wpar, woff=woff:
                               e.dma_start(
                                   out=watt_scr[:, woff:woff + 2048],
                                   in_=qkw[wpar][:]),
                               waits=[ld_rec], inc=s_ast[wpar], amt=16)
                            a2["qk_stw"].append((s_ast[wpar], s_ast[wpar].n))
                        else:
                            if t == 1 and u == a2["qk_u0"]:
                                wts += [(s, s.n) for s in s_ast]
                            em("sync", lambda e, wpar=wpar, woff=woff:
                               e.dma_start(
                                   out=qkw[wpar][:],
                                   in_=watt_scr[:, woff:woff + 2048]),
                               waits=wts, inc=s_qkl[wpar], amt=16)
                            ld_rec = (s_qkl[wpar], s_qkl[wpar].n)
                        bank = (psA, psB)[u % 2]
                        wts2 = [ld_rec, (s_tpe, xnt_done)]
                        if len(a2["qk_ev"]) >= 2:
                            wts2.append((s_sc, a2["qk_ev"][-2]))
                        for c in range(DC):
                            em("tensor", lambda e, c=c, wpar=wpar, bank=bank,
                               XnT=XnT: e.matmul(
                                   bank, qkw[wpar][:, c * P:(c + 1) * P],
                                   XnT[:, c * TT:(c + 1) * TT],
                                   start=(c == 0), stop=(c == DC - 1)),
                               waits=wts2 if c == 0 else [],
                               inc=s_tp if c == DC - 1 else None)
                        a2["qk_done"].append(s_tp.n)
                        sp = u % 2
                        wts3 = [(s_tp, s_tp.n)]
                        if len(a2["qk_st"]) >= 2:
                            wts3.append((s_stq[sp], a2["qk_st"][-2]))
                        qsc = ISQ if wi == 0 else 1.0
                        em("scalar", lambda e, sp=sp, dc=dc, bank=bank,
                           bias_t=bias_t, qsc=qsc: e.activation(
                               qstage[sp][:], bank, AF.Identity,
                               bias=bias_t[:, dc:dc + 1], scale=qsc),
                           waits=wts3, inc=s_sc)
                        a2["qk_ev"].append(s_sc.n)
                        em("scalar", lambda e, sp=sp, dc=dc, t=t, scr=scr:
                           e.dma_start(
                               out=scr[dc * P:(dc + 1) * P,
                                       t * TT:(t + 1) * TT],
                               in_=qstage[sp][:]),
                           waits=[(s_sc, s_sc.n)], inc=s_stq[sp], amt=16)
                        a2["qk_st"].append(s_stq[sp].n)

                for dg in range(4):
                    u = a2["v_u"]
                    a2["v_u"] += 1
                    wpar = u % 2
                    woff = 2 * 32768 + dg * 8192
                    wts = []
                    if len(a2["v_done"]) >= 2:
                        wts.append((s_tp, a2["v_done"][-2]))
                    if t == 0:
                        if len(a2["v_stw"]) >= 2:
                            wts.append(a2["v_stw"][-2])
                        em("gpsimd", lambda e, wpar=wpar, dg=dg: e.dma_start(
                            out=vw[wpar][:].rearrange("p (c n) -> p c n", c=DC),
                            in_=wv_h.rearrange("(c p) n -> p c n", p=P)[
                                :, :, dg * 512:(dg + 1) * 512]),
                           waits=wts, inc=s_vw[wpar], amt=16)
                        ld_rec = (s_vw[wpar], s_vw[wpar].n)
                        em("sync", lambda e, wpar=wpar, woff=woff: e.dma_start(
                            out=watt_scr[:, woff:woff + 8192], in_=vw[wpar][:]),
                           waits=[ld_rec], inc=s_vst[wpar], amt=16)
                        a2["v_stw"].append((s_vst[wpar], s_vst[wpar].n))
                    else:
                        if t == 1 and dg == 0:
                            wts += [(s_vst[0], s_vst[0].n),
                                    (s_vst[1], s_vst[1].n)]
                        em("sync", lambda e, wpar=wpar, woff=woff: e.dma_start(
                            out=vw[wpar][:], in_=watt_scr[:, woff:woff + 8192]),
                           waits=wts, inc=s_vl[wpar], amt=16)
                        ld_rec = (s_vl[wpar], s_vl[wpar].n)
                    unit_last = None
                    for tg in range(NG):
                        bank = (psC, psD)[(u * NG + tg) % 2]
                        wts2 = [ld_rec, (s_tpe, xnt_done)] if tg == 0 else []
                        if len(a2["v_ev"]) >= 2:
                            wts2.append((s_av, a2["v_ev"][-2]))
                        for c in range(DC):
                            em("tensor", lambda e, c=c, tg=tg, wpar=wpar,
                               bank=bank, XnT=XnT: e.matmul(
                                   bank,
                                   XnT.rearrange("p (c n) -> p c n", c=DC)[
                                       :, c, tg * P:(tg + 1) * P],
                                   vw[wpar][:, c * 512:(c + 1) * 512],
                                   start=(c == 0), stop=(c == DC - 1)),
                               waits=wts2 if c == 0 else [],
                               inc=s_tp if c == DC - 1 else None)
                        unit_last = s_tp.n
                        sp = (u * NG + tg) % 2
                        wts3 = [(s_tp, s_tp.n)]
                        if len(a2["v_st"]) >= 2:
                            wts3.append((s_stv[sp], a2["v_st"][-2]))
                        em("vector", lambda e, sp=sp, dg=dg, bank=bank:
                           e.tensor_tensor(
                               vstage[sp][:], bank,
                               bvb[:, dg * 512:(dg + 1) * 512], op=ALU.add),
                           waits=wts3, inc=s_av)
                        a2["v_ev"].append(s_av.n)
                        em("scalar", lambda e, sp=sp, dg=dg, tg=tg, t=t:
                           e.dma_start(
                               out=v_scr[t * TT + tg * P:
                                         t * TT + (tg + 1) * P,
                                         dg * 512:(dg + 1) * 512],
                               in_=vstage[sp][:]),
                           waits=[(s_av, s_av.n)], inc=s_stv[sp], amt=16)
                        a2["v_st"].append(s_stv[sp].n)
                    a2["v_done"].append(unit_last)

        def attn_pass():
            qth = [bufW1[:, 0:1024].bitcast(BF16),
                   bufW1[:, 3072:4096].bitcast(BF16)]
            kth = [bufW1[:, 1024:2048].bitcast(BF16),
                   bufW1[:, 4096:5120].bitcast(BF16)]
            vth = [bufW1[:, 2048:3072].bitcast(BF16),
                   bufW1[:, 5120:6144].bitcast(BF16)]
            S2 = [bufA[:, 0:2048], bufA[:, 2048:4096]]
            A2b = [bufB[:, 0:1024].bitcast(BF16),
                   bufB[:, 1024:2048].bitcast(BF16),
                   bufB[:, 2048:3072].bitcast(BF16)]
            AT2 = [bufB[:, 3072:4096].bitcast(BF16),
                   bufB[:, 4096:5120].bitcast(BF16),
                   bufB[:, 5120:6144].bitcast(BF16)]
            avst = bufE[:, 64:320].bitcast(BF16)
            nmax2 = [bufE[:, 8:9], bufE[:, 9:10], bufE[:, 14:15]]
            lsum2 = [bufE[:, 10:11], bufE[:, 11:12], bufE[:, 15:16]]
            rinv2 = [bufE[:, 12:13], bufE[:, 13:14], bufE[:, 16:17]]
            rrow2 = [bufE[0:1, 1216:1344], bufE[0:1, 1472:1600 - 64]
                     if False else bufE[0:1, 1344:1472]]
            rrow2 = [bufE[0:1, 1216:1344], bufE[0:1, 1344:1472]]
            rbc2 = [bufE[:, 704:832], bufE[:, 832:960]]
            bs = dict(cons=[], avev=[], avst=[], uc=0, bankfree={},
                      exv2={}, atpe2={}, avpe2={}, rrowd2={}, rbcrd2={})
            pend = [None]
            pend2 = [None]
            pend3 = [None]
            stq_tot = [(s_stq[0], s_stq[0].n), (s_stq[1], s_stq[1].n),
                       (s_stv[0], s_stv[0].n), (s_stv[1], s_stv[1].n)]

            def emit_atav(pu):
                iq = pu["iq"]
                par = pu["par"]
                p3 = pu["p3"]
                si = pu["si"]
                hh = pu["hh"]
                ev_vals = []
                for kc in range(iq + 1):
                    slot = kc % 8
                    wts2 = []
                    if kc == 0:
                        wts2.append((s_h, pu["sa_done"]))
                    if slot == 0 and bs.get("atev") is not None:
                        wts2.append((s_tpe, bs["atev"]))
                    em("tensor", lambda e, kc=kc, slot=slot, p3=p3:
                       e.transpose(
                           psG[:, slot * P:(slot + 1) * P],
                           A2b[p3][:, kc * P:(kc + 1) * P], cs_identb[:]),
                       waits=wts2,
                       inc=s_tp if (slot == 7 or kc == iq) else None)
                    if slot == 7 or kc == iq:
                        lo = kc - slot
                        em("scalar", lambda e, lo=lo, slot=slot, p3=p3:
                           e.copy(
                               AT2[p3][:, lo * P:(lo + slot + 1) * P],
                               psG[:, 0:(slot + 1) * P]),
                           waits=[(s_tp, s_tp.n)], inc=s_tpe)
                        bs["atev"] = s_tpe.n
                        ev_vals.append(s_tpe.n)
                bs["atpe2"][p3] = s_tp.n
                bank = (psE, psF)[pu["uc"] % 2]
                wts3 = [(s_tpe, ev_vals[0])]
                if len(bs["avev"]) >= 2:
                    wts3.append((s_evA, bs["avev"][-2]))
                for kc in range(iq + 1):
                    if kc == 0:
                        wav = wts3
                    elif kc % 8 == 0:
                        wav = [(s_tpe, ev_vals[kc // 8])]
                    else:
                        wav = []
                    em("tensor", lambda e, kc=kc, par=par, bank=bank,
                       iq=iq, p3=p3: e.matmul(
                           bank[:, 0:P],
                           vth[par][:, kc * P:(kc + 1) * P],
                           AT2[p3][:, kc * P:(kc + 1) * P],
                           start=(kc == 0), stop=(kc == iq)),
                       waits=wav,
                       inc=s_tp if kc == iq else None)
                av_done = s_tp.n
                bs["avpe2"][p3] = av_done
                sslot = iq % 4
                wts4 = [(s_tp, av_done)]
                if sslot == 0 and len(bs["avst"]) >= 1:
                    wts4.append((s_sta, bs["avst"][-1]))
                em("scalar", lambda e, sslot=sslot, bank=bank:
                   e.copy(avst[:, sslot * P:(sslot + 1) * P], bank[:, 0:P]),
                   waits=wts4, inc=s_evA)
                bs["avev"].append(s_evA.n)
                if sslot == 3:
                    lo = iq - 3
                    em("sync", lambda e, lo=lo, hh=hh, si=si:
                       e.dma_start(
                           out=avt_scr[hh * DH:(hh + 1) * DH,
                                       si * S + lo * P:
                                       si * S + (lo + 4) * P],
                           in_=avst[:]),
                       waits=[(s_evA, s_evA.n)], inc=s_sta, amt=16)
                    bs["avst"].append(s_sta.n)
                if iq == S // P - 1:
                    bs["cons"].append(s_evA.n)

            for si in range(1 if lite else NSEQ):
                for hh in range(3 if lite else NH):
                    hi = si * NH + hh
                    par = hi % 2
                    wts = list(stq_tot) if hi == 0 else []
                    if hi >= 2 and len(bs["cons"]) >= 1:
                        wts.append((s_evA, bs["cons"][-1]))
                    em("scalar", lambda e, par=par, hh=hh, si=si: e.dma_start(
                        out=qth[par][:],
                        in_=qt_scr[hh * DH:(hh + 1) * DH, si * S:(si + 1) * S]),
                       waits=wts, inc=s_qkp[par], amt=16)
                    em("scalar", lambda e, par=par, hh=hh, si=si: e.dma_start(
                        out=kth[par][:],
                        in_=kt_scr[hh * DH:(hh + 1) * DH, si * S:(si + 1) * S]),
                       inc=s_qkp[par], amt=16)
                    em("scalar", lambda e, par=par, hh=hh, si=si: e.dma_start(
                        out=vth[par][:].rearrange("p (c n) -> p c n", c=DC),
                        in_=v_scr[si * S:(si + 1) * S,
                                  hh * DH:(hh + 1) * DH].rearrange(
                                      "(c p) n -> p c n", p=P)),
                       inc=s_qkp[par], amt=16)
                    ld_hi = s_qkp[par].n

                    for iq in range(S // P):
                        uc = bs["uc"]
                        bs["uc"] += 1
                        p2 = uc % 2
                        p3 = uc % 3
                        W = (iq + 1) * P
                        nkb = (W + 511) // 512
                        first_extra = [(s_qkp[par], ld_hi)] if iq == 0 else []
                        mm_done = []
                        for kb in range(nkb):
                            wdt = min(512, W - kb * 512)
                            wts = list(first_extra) if kb == 0 else []
                            if kb in bs["bankfree"]:
                                wts.append(bs["bankfree"][kb])
                            em("tensor", lambda e, kb=kb, iq=iq, par=par,
                               wdt=wdt: e.matmul(
                                   ps_big[:, kb * 512:kb * 512 + wdt],
                                   qth[par][:, iq * P:(iq + 1) * P],
                                   kth[par][:, kb * 512:kb * 512 + wdt],
                                   start=True, stop=True),
                               waits=wts, inc=s_sc)
                            mm_done.append(s_sc.n)
                        ev_guard = [(s_evA, bs["exv2"][p2])] if p2 in bs["exv2"] else []
                        cp_act = 0
                        mk_done = None
                        for kb in range(nkb):
                            wdt = min(512, W - kb * 512)
                            plain = wdt - P if kb == nkb - 1 else wdt
                            wev = [(s_sc, mm_done[kb])] + (ev_guard if kb == 0 else [])
                            if plain > 0:
                                if kb % 2 == 0:
                                    em("scalar", lambda e, kb=kb, p2=p2,
                                       plain=plain: e.copy(
                                           S2[p2][:, kb * 512:kb * 512 + plain],
                                           ps_big[:, kb * 512:kb * 512 + plain]),
                                       waits=wev, inc=s_evA)
                                    cp_act = s_evA.n
                                    bs["bankfree"][kb] = (s_evA, cp_act)
                                else:
                                    em("vector", lambda e, kb=kb, p2=p2,
                                       plain=plain: e.tensor_copy(
                                           S2[p2][:, kb * 512:kb * 512 + plain],
                                           ps_big[:, kb * 512:kb * 512 + plain]),
                                       waits=wev, inc=s_h)
                                    cp_dve = s_h.n
                                    bs["bankfree"][kb] = (s_h, cp_dve)
                            if kb == nkb - 1:
                                wmk = [(s_sc, mm_done[kb])]
                                if kb == 0:
                                    wmk += ev_guard
                                if plain > 0 and kb % 2 == 0:
                                    wmk.append((s_evA, cp_act))
                                em("vector", lambda e, iq=iq, p2=p2:
                                   e.tensor_tensor(
                                       S2[p2][:, iq * P:(iq + 1) * P],
                                       ps_big[:, iq * P:(iq + 1) * P],
                                       cs_mask[:], op=ALU.add),
                                   waits=wmk, inc=s_h)
                                mk_done = s_h.n
                                bs["bankfree"][kb] = (s_h, mk_done)
                        pend2b = None
                        if pend[0] is not None:
                            pv = pend[0]
                            vp2 = pv["p2"]
                            vp3 = pv["p3"]
                            wex = [(s_h, pv["nm_done"])]
                            if vp3 in bs["atpe2"]:
                                wex.append((s_tp, bs["atpe2"][vp3]))
                            em("scalar", lambda e, W=pv["W"], p2=vp2, p3=vp3:
                               e.activation(
                                   A2b[p3][:, 0:W], S2[p2][:, 0:W], AF.Exp,
                                   bias=nmax2[p3], scale=1.0,
                                   accum_out=lsum2[p3]),
                               waits=wex, inc=s_evA)
                            pv["ex_done"] = s_evA.n
                            bs["exv2"][vp2] = pv["ex_done"]
                            em("vector", lambda e, p3=vp3: e.reciprocal(
                                rinv2[p3], lsum2[p3]),
                               waits=[(s_evA, pv["ex_done"])])
                            em("vector", lambda e, W=pv["W"], p3=vp3:
                               e.tensor_scalar_mul(
                                   A2b[p3][:, 0:W], A2b[p3][:, 0:W],
                                   rinv2[p3]), inc=s_h)
                            pv["sa_done"] = s_h.n
                            pend2b = pv
                        if pend3[0] is not None:
                            emit_atav(pend3[0])
                        pend3[0] = pend2[0]
                        pend2[0] = pend2b
                        wmx = [(s_h, mk_done)]
                        if cp_act:
                            wmx.append((s_evA, cp_act))
                        em("vector", lambda e, W=W, p2=p2, p3=p3: e.reduce_max(
                            nmax2[p3], S2[p2][:, 0:W], axis=AX.X, negate=True),
                           waits=wmx, inc=s_h)
                        nm_done = s_h.n
                        pend[0] = dict(iq=iq, par=par, p2=p2, p3=p3, si=si,
                                       hh=hh, uc=uc, nm_done=nm_done, W=W)

            if pend3[0] is not None:
                emit_atav(pend3[0])
                pend3[0] = None
            if pend2[0] is not None:
                emit_atav(pend2[0])
                pend2[0] = None
            if pend[0] is not None:
                pv = pend[0]
                vp2 = pv["p2"]
                vp3 = pv["p3"]
                wex = [(s_h, pv["nm_done"])]
                if vp3 in bs["atpe2"]:
                    wex.append((s_tp, bs["atpe2"][vp3]))
                em("scalar", lambda e, W=pv["W"], p2=vp2, p3=vp3: e.activation(
                    A2b[p3][:, 0:W], S2[p2][:, 0:W], AF.Exp,
                    bias=nmax2[p3], scale=1.0, accum_out=lsum2[p3]),
                   waits=wex, inc=s_evA)
                pv["ex_done"] = s_evA.n
                em("vector", lambda e, p3=vp3: e.reciprocal(
                    rinv2[p3], lsum2[p3]), waits=[(s_evA, pv["ex_done"])])
                em("vector", lambda e, W=pv["W"], p3=vp3: e.tensor_scalar_mul(
                    A2b[p3][:, 0:W], A2b[p3][:, 0:W], rinv2[p3]), inc=s_h)
                pv["sa_done"] = s_h.n
                emit_atav(pv)

        def oproj_pass():
            bob = bufB[:, 0:2048]
            tmp = bufB[:, 2048:2560]
            em("sync", lambda e: e.dma_start(out=bob, in_=bob_h[:]),
               waits=[(s_stx, s_stx.n)], inc=s_ld, amt=16)
            ldv = s_ld.n
            wow = [bufW1[:, 0:4096].bitcast(BF16),
                   bufW1[:, 4096:8192].bitcast(BF16)]
            avt = bufC[:].bitcast(BF16)
            c1 = dict(u=0, done=[], ev=[])
            c1["stw"] = []
            for t in range(2 if lite else NTILE):
                wl1 = [(s_ld, ldv)]
                if t == 0:
                    wl1 += [(s_stx, s_stx.n), (s_sta, s_sta.n)]
                if c1.get("bufA_free"):
                    wl1.append((s_av, c1["bufA_free"]))
                em("scalar", lambda e, t=t: e.dma_start(
                    out=bufA[:].rearrange("p (g f) -> p g f", g=NG),
                    in_=x1_scr[t * TT:(t + 1) * TT, :].rearrange(
                        "(g p) f -> p g f", p=P)),
                   waits=wl1, inc=s_x, amt=16)
                em("scalar", lambda e, t=t: e.dma_start(
                    out=avt[:].rearrange("p (c n) -> p c n", c=DC),
                    in_=avt_scr.rearrange("(c p) n -> p c n", p=P)[
                        :, :, t * TT:(t + 1) * TT]),
                   inc=s_x, amt=16)
                x_ready = s_x.n
                for dg in range(4):
                    u = c1["u"]
                    c1["u"] += 1
                    wpar = u % 2
                    woff = 3 * 32768 + dg * 8192
                    wts = []
                    if len(c1["done"]) >= 2:
                        wts.append((s_tp, c1["done"][-2]))
                    if t == 0:
                        if len(c1["stw"]) >= 2:
                            wts.append(c1["stw"][-2])
                        em("gpsimd", lambda e, wpar=wpar, dg=dg: e.dma_start(
                            out=wow[wpar][:].rearrange("p (c n) -> p c n", c=DC),
                            in_=wo_h.rearrange("(c p) n -> p c n", p=P)[
                                :, :, dg * 512:(dg + 1) * 512]),
                           waits=wts, inc=s_vw[wpar], amt=16)
                        ld_rec = (s_vw[wpar], s_vw[wpar].n)
                        em("sync", lambda e, wpar=wpar, woff=woff: e.dma_start(
                            out=watt_scr[:, woff:woff + 8192], in_=wow[wpar][:]),
                           waits=[ld_rec], inc=s_vst[wpar], amt=16)
                        c1["stw"].append((s_vst[wpar], s_vst[wpar].n))
                    else:
                        if t == 1 and dg == 0:
                            wts += [(s_vst[0], s_vst[0].n),
                                    (s_vst[1], s_vst[1].n)]
                        em("sync", lambda e, wpar=wpar, woff=woff: e.dma_start(
                            out=wow[wpar][:], in_=watt_scr[:, woff:woff + 8192]),
                           waits=wts, inc=s_vl[wpar], amt=16)
                        ld_rec = (s_vl[wpar], s_vl[wpar].n)
                    unit_last = None
                    for tg in range(NG):
                        r2 = u * NG + tg
                        bank = (psA, psB)[r2 % 2]
                        wts2 = [ld_rec, (s_x, x_ready)] if tg == 0 else []
                        if len(c1["ev"]) >= 2:
                            wts2.append((s_av, c1["ev"][-2]))
                        for c in range(DC):
                            em("tensor", lambda e, c=c, tg=tg, wpar=wpar,
                               bank=bank, avt=avt: e.matmul(
                                   bank,
                                   avt.rearrange("p (c n) -> p c n", c=DC)[
                                       :, c, tg * P:(tg + 1) * P],
                                   wow[wpar][:, c * 512:(c + 1) * 512],
                                   start=(c == 0), stop=(c == DC - 1)),
                               waits=wts2 if c == 0 else [],
                               inc=s_tp if c == DC - 1 else None)
                        unit_last = s_tp.n
                        em("vector", lambda e, dg=dg, bank=bank:
                           e.tensor_tensor(
                               tmp, bank, bob[:, dg * 512:(dg + 1) * 512],
                               op=ALU.add), waits=[(s_tp, unit_last)])
                        sl = slice(tg * D + dg * 512, tg * D + (dg + 1) * 512)
                        wadd = []
                        if dg == 0 and tg == 0 and c1.get("st_prev"):
                            wadd.append((s_stx, c1["st_prev"]))
                        em("vector", lambda e, sl=sl: e.tensor_tensor(
                            bufW2[:, sl], bufA[:, sl], tmp, op=ALU.add),
                           waits=wadd, inc=s_av)
                        c1["ev"].append(s_av.n)
                    c1["done"].append(unit_last)
                em("sync", lambda e, t=t: e.dma_start(
                    out=x2_scr[t * TT:(t + 1) * TT, :].rearrange(
                        "(g p) f -> p g f", p=P),
                    in_=bufW2[:, 0:8192].rearrange("p (g f) -> p g f", g=NG)),
                   waits=[(s_av, s_av.n)], inc=s_stx, amt=16)
                c1["st_prev"] = s_stx.n
                c1["bufA_free"] = s_av.n

        ffn_pass(x_h, x1_scr, precast=True)
        barrier()
        qkv_pass()
        barrier()
        attn_pass()
        barrier()
        oproj_pass()
        barrier()
        ffn_pass(x2_scr, y_h, precast=False)

        em("gpsimd", lambda e: e.nop(), waits=[(s_stx, s_stx.n)])

        blk = st.enter_context(nc.Block())

        def mk_body(lst):
            def body(eng):
                for (waits, fn, inch, amt) in lst:
                    for (sm, v) in waits:
                        if v > 0:
                            eng.wait_ge(sm.h, v)
                    ins = fn(eng)
                    if inch is not None:
                        ins.then_inc(inch, amt)
            return body
        for ename in ENGINES:
            getattr(blk, ename)(mk_body(items[ename]))

    return nc


def _consts():
    ident = np.eye(P, dtype=np.float32)
    identb = np.eye(P, dtype=np.float32).astype(ml_dtypes.bfloat16)
    sel8 = np.zeros((8, 8 * P), dtype=np.float32)
    for e in range(8):
        sel8[e, e * P:(e + 1) * P] = 1.0
    mask = np.where(np.arange(P)[:, None] >= np.arange(P)[None, :],
                    np.float32(0.0), np.float32(-1e30)).astype(np.float32)
    ones1 = np.ones((1, P), dtype=np.float32)
    onesb = np.ones((P, TT), dtype=np.float32).astype(ml_dtypes.bfloat16)
    return ident, identb, sel8, mask, ones1, onesb


def make_in_maps(inp):
    ident, identb, sel8, mask, ones1, onesb = _consts()
    asc = np.ascontiguousarray
    gate = np.asarray(inp["L0_ffn_gate"])
    ebias = np.asarray(inp["L0_ffn_expert_bias"])
    common = dict(
        w1t=asc(np.asarray(inp["L0_ffn_w1"]).transpose(0, 2, 1)),
        w3t=asc(np.asarray(inp["L0_ffn_w3"]).transpose(0, 2, 1)),
        w2t=asc(np.asarray(inp["L0_ffn_w2"]).transpose(0, 2, 1)),
        sw1t=asc(np.asarray(inp["L0_ffn_shared_w1"]).T),
        sw3t=asc(np.asarray(inp["L0_ffn_shared_w3"]).T),
        sw2t=asc(np.asarray(inp["L0_ffn_shared_w2"]).T),
        wq=np.asarray(inp["L0_attn_Wq"]), wk=np.asarray(inp["L0_attn_Wk"]),
        wv=np.asarray(inp["L0_attn_Wv"]), wo=np.asarray(inp["L0_attn_Wo"]),
        identf=ident, identb=identb, sel8=sel8, mask=mask,
        gatei=asc(gate.T.reshape(DC, P, 8).transpose(1, 0, 2).reshape(P, DC * 8)),
        bias32=asc(np.broadcast_to(np.tile(ebias, 4)[None, :], (P, 32))),
        bq16=asc(np.asarray(inp["L0_attn_bq"]).reshape(DC, P).T * ISQ),
        bk16=asc(np.asarray(inp["L0_attn_bk"]).reshape(DC, P).T),
        lngb=asc(np.broadcast_to(np.asarray(inp["L0_attn_ln_g"])[None, :], (P, D))),
        lnbb=asc(np.broadcast_to(np.asarray(inp["L0_attn_ln_b"])[None, :], (P, D))),
        bvb=asc(np.broadcast_to(np.asarray(inp["L0_attn_bv"])[None, :], (P, D))),
        bob=asc(np.broadcast_to(np.asarray(inp["L0_attn_bo"])[None, :], (P, D))),
        ones1=ones1, onesb=onesb,
    )
    X = np.asarray(inp["L0_attn_X"]).reshape(16 * S, D)
    in_maps = []
    for c in range(8):
        m = dict(common)
        m["x"] = asc(X[c * NT:(c + 1) * NT])
        in_maps.append(m)
    return in_maps


def kernel(**inp):
    in_maps = make_in_maps(inp)
    nc = bass.Bass("TRN2", target_bir_lowering=False, debug=False)
    build(nc)
    res = run_bass_kernel_spmd(nc, in_maps, core_ids=list(range(8)))
    out = np.concatenate([np.asarray(r["y"]) for r in res.results], axis=0)
    return out.reshape(16, S, D)
